# version 12
# baseline (speedup 1.0000x reference)
import numpy as np
import ml_dtypes
from contextlib import ExitStack

import concourse.bass as bass
import concourse.mybir as mybir
from concourse.bass_utils import run_bass_kernel_spmd

F32 = mybir.dt.float32
BF16 = mybir.dt.bfloat16
AF = mybir.ActivationFunctionType
ALU = mybir.AluOpType
AX = mybir.AxisListType

T = 2048
DM = 4096
KC = 32
NTC = 16
NTR = 4
NKV = 8
GQ = 4
NE = 8
NOR = 8
SCALE = float(128.0 ** -0.5)
EPS = 1e-5
MCOL = 2432

CAP = 512
NSLOT = NE * CAP

PHASES = 4
DEBUG = False
P2_STAGE = "full"


class Em:

    def __init__(self, nc, role, eng, sem):
        self.nc = nc
        self.role = role
        self.eng = eng
        self.sem = sem
        self.c = dict(pe=0, act=0, dve=0, sx=0, sc=0, sg=0, out=0)
        self.rel = {}

    def w(self, role, sname, val):
        if val is None or val <= 0:
            return
        if self.role == role:
            self.eng.wait_ge(self.sem[sname], val)

    def wcur(self, role, sname):
        self.w(role, sname, self.c[sname])

    def wrel(self, role, key):
        r = self.rel.get(key)
        if r is not None:
            self.w(role, r[0], r[1])

    def setrel(self, key, sname, val):
        self.rel[key] = (sname, val)

    def _inc(self, inst, sname, n):
        if sname is None:
            return None
        self.c[sname] += n
        if inst is not None:
            inst.then_inc(self.sem[sname], n)
        return self.c[sname]

    def mm(self, out, lhsT, rhs, start, stop, inc=False):
        inst = None
        if self.role == "tensor":
            inst = self.nc.tensor.matmul(out, lhsT, rhs, start=start, stop=stop,
                                         skip_group_check=True)
        return self._inc(inst, "pe" if inc else None, 1)

    def tr(self, out, in_, ident, inc=True):
        inst = None
        if self.role == "tensor":
            inst = self.nc.tensor.transpose(out, in_, ident)
        return self._inc(inst, "pe" if inc else None, 1)

    def act(self, out, in_, func, bias=0.0, scale=1.0, accum_out=None, inc=True):
        inst = None
        if self.role == "scalar":
            inst = self.nc.scalar.activation(out, in_, func, bias=bias, scale=scale,
                                             accum_out=accum_out)
        return self._inc(inst, "act" if inc else None, 1)

    def dve(self, method, *args, inc=True, **kw):
        inst = None
        if self.role == "vector":
            inst = getattr(self.nc.vector, method)(*args, **kw)
        return self._inc(inst, "dve" if inc else None, 1)

    def dma(self, qrole, out, in_, sname):
        inst = None
        if self.role == qrole:
            eng = {"sync": self.nc.sync, "scalar": self.nc.scalar,
                   "gpsimd": self.nc.gpsimd}[qrole]
            inst = eng.dma_start(out=out, in_=in_)
        return self._inc(inst, sname, 16)

    def idma(self, out, out_off, in_, in_off):
        inst = None
        if self.role == "gpsimd":
            inst = self.nc.gpsimd.indirect_dma_start(out, out_off, in_, in_off)
        return self._inc(inst, "sg", 16)


def build_nc():
    nc = bass.Bass("TRN2", target_bir_lowering=False)

    x_d = nc.dram_tensor("x", [T, DM], F32, kind="ExternalInput").ap()
    wqkv_d = nc.dram_tensor("w_qkv", [DM, 6144], F32, kind="ExternalInput").ap()
    wo_d = nc.dram_tensor("w_o", [DM, DM], F32, kind="ExternalInput").ap()
    gwT_d = nc.dram_tensor("gwT", [128, KC + 1, NE], F32, kind="ExternalInput").ap()
    wjT_d = nc.dram_tensor("wjT", [NE, DM + 128, DM], F32, kind="ExternalInput").ap()
    sqd_d = nc.dram_tensor("sqd", [128, MCOL], F32, kind="ExternalInput").ap()
    lt_d = nc.dram_tensor("ltones", [128, 256], BF16, kind="ExternalInput").ap()
    jb_d = nc.dram_tensor("jb", [128, NE], F32, kind="ExternalInput").ap()
    iota_d = nc.dram_tensor("iota16", [128, NTC], mybir.dt.uint32, kind="ExternalInput").ap()
    ident_d = nc.dram_tensor("ident_i", [128, 128], BF16, kind="ExternalInput").ap()
    onesr_d = nc.dram_tensor("ones_row_i", [1, 128], BF16, kind="ExternalInput").ap()
    onesc_d = nc.dram_tensor("ones_col_i", [128, 1], BF16, kind="ExternalInput").ap()

    y_d = nc.dram_tensor("y", [T, DM], F32, kind="ExternalOutput").ap()
    ikind = "ExternalOutput" if DEBUG else "Internal"
    xnT_d = nc.dram_tensor("xnT_s", [DM, T], BF16, kind=ikind).ap()
    kT_d = nc.dram_tensor("kT_s", [NKV * 128, T], BF16, kind=ikind).ap()
    v_d = nc.dram_tensor("v_s", [T, NKV * 128], BF16, kind=ikind).ap()
    sT_d = nc.dram_tensor("sT_s", [T, DM], BF16, kind=ikind).ap()
    xt_d = nc.dram_tensor("xtok_s", [T, DM], BF16, kind="Internal").ap()
    ffn_d = nc.dram_tensor("ffn_s", [NSLOT, DM], BF16, kind=ikind).ap()
    gl_d = nc.dram_tensor("glist_s", [NSLOT, 1], mybir.dt.uint32, kind=ikind).ap()
    masks_d = (nc.dram_tensor("masks_s", [128, NTC, NE], BF16, kind="ExternalOutput").ap()
               if DEBUG else None)

    with ExitStack() as st:
        E = st.enter_context
        big = E(nc.sbuf_tensor("big", [128, 65536], BF16))
        aux = E(nc.sbuf_tensor("aux", [128, 16384], BF16))
        kvb = E(nc.sbuf_tensor("kvb", [128, 4096], BF16))
        shf = E(nc.sbuf_tensor("shf", [128, 4096], F32))
        metb = E(nc.sbuf_tensor("metb", [128, MCOL], BF16))
        smb = E(nc.sbuf_tensor("smb", [128, 4096], BF16))
        stf = E(nc.sbuf_tensor("stf", [128, 8], F32))
        masks = E(nc.sbuf_tensor("masks", [128, NTC, NE], BF16))
        lt_sb = E(nc.sbuf_tensor("lt_sb", [128, 256], BF16))
        jb_sb = E(nc.sbuf_tensor("jb_sb", [128, NE], F32))
        slotf = E(nc.sbuf_tensor("slotf", [128, NE], F32))
        slotu = E(nc.sbuf_tensor("slotu", [128, NTC], mybir.dt.uint32))
        iota_sb = E(nc.sbuf_tensor("iota_sb", [128, NTC], mybir.dt.uint32))
        zsb = E(nc.sbuf_tensor("zsb", [128, 32], mybir.dt.uint32))
        slotw = E(nc.sbuf_tensor("slotw", [128, 4], mybir.dt.uint32))
        glall = E(nc.sbuf_tensor("glall", [128, 32, 1], mybir.dt.uint32))
        recbf = E(nc.sbuf_tensor("recbf", [1, 512], BF16))
        out4 = E(nc.sbuf_tensor("out4", [128, 512], F32))
        gw_bf = E(nc.sbuf_tensor("gw_bf", [128, (KC + 1) * NE], BF16))
        ident = E(nc.sbuf_tensor("ident", [128, 128], BF16))
        onesr = E(nc.sbuf_tensor("onesr", [1, 128], BF16))
        onesc = E(nc.sbuf_tensor("onesc", [128, 1], BF16))

        ps_a = E(nc.psum_tensor("ps_a", [128, 512], F32))
        ps_b = E(nc.psum_tensor("ps_b", [128, 512], F32))
        ps_c = E(nc.psum_tensor("ps_c", [128, 512], F32))
        ps_d = E(nc.psum_tensor("ps_d", [128, 512], F32))
        ps_e = E(nc.psum_tensor("ps_e", [128, 512], F32))
        ps_den = E(nc.psum_tensor("ps_den", [1, 512], F32))
        ps_t0 = E(nc.psum_tensor("ps_t0", [128, 128], BF16))
        ps_t1 = E(nc.psum_tensor("ps_t1", [128, 128], BF16))

        sems = {}
        for sn in ("pe", "act", "dve", "sx", "sc", "sg", "out"):
            sems[sn] = E(nc.semaphore(sn))
        block = E(nc.Block())

        def fin(P: Em):
            P.w("sync", "out", P.c["out"])
            P.w("sync", "sx", P.c["sx"])
            P.w("sync", "sc", P.c["sc"])
            P.w("sync", "sg", P.c["sg"])
            P.w("sync", "pe", P.c["pe"])
            P.w("sync", "dve", P.c["dve"])
            P.w("sync", "act", P.c["act"])

        def body(P: Em):
            ps_ab = (ps_a, ps_b)
            ps_tr2 = (ps_t0, ps_t1)

            P.dma("sync", ident[:], ident_d, "sx")
            c_onesr = P.dma("sync", onesr[:], onesr_d, "sx")
            P.dma("sync", onesc[:], onesc_d, "sx")
            c_gwbf = P.dma("gpsimd", gw_bf[:], gwT_d.rearrange("p c e -> p (c e)"), "sg")
            P.dma("sync", lt_sb[:], lt_d, "sx")
            P.dma("sync", jb_sb[:], jb_d, "sx")
            c_iota = P.dma("sync", iota_sb[:], iota_d, "sx")
            d_z = P.dve("memset", zsb[:], 0)
            P.w("sync", "dve", d_z)
            c_glz = P.dma("sync", gl_d.rearrange("(p c) o -> p (c o)", c=32),
                          zsb[:], "sx")

            x_users = None
            kvb_users_pe = 0
            a_sq = 0

            for tc in range(NTC):
                if x_users is not None:
                    P.w("sync", "dve", x_users[0])
                c_x = P.dma("sync", shf[:], x_d[tc * 128:(tc + 1) * 128, :], "sx")
                if tc == 0:
                    prelude_sx = c_x
                P.w("vector", "sx", c_x)
                d1 = P.dve("reduce_sum", stf[:, 0:1], shf[:], axis=AX.X)
                P.w("scalar", "sx", c_x)
                a_sq = P.act(aux[:, 0:4096], shf[:], AF.Square, accum_out=stf[:, 1:2])
                P.setrel("aux", "act", a_sq)
                P.w("vector", "act", a_sq)
                P.w("vector", "dve", d1)
                d2 = P.dve("tensor_scalar_mul", stf[:, 2:3], stf[:, 0:1], 1.0 / DM)
                P.dve("tensor_scalar_mul", stf[:, 3:4], stf[:, 1:2], 1.0 / DM, inc=False)
                P.w("vector", "dve", d2)
                d4 = P.dve("tensor_mul", stf[:, 5:6], stf[:, 2:3], stf[:, 2:3])
                P.w("vector", "dve", d4)
                d5 = P.dve("tensor_sub", stf[:, 3:4], stf[:, 3:4], stf[:, 5:6])
                P.w("vector", "dve", d5)
                d_var = P.dve("tensor_scalar_add", stf[:, 3:4], stf[:, 3:4], EPS)
                P.w("scalar", "dve", d_var)
                a_ln = P.act(stf[:, 5:6], stf[:, 3:4], AF.Ln)
                P.w("scalar", "act", a_ln)
                a_rstd = P.act(stf[:, 4:5], stf[:, 5:6], AF.Exp, scale=-0.5)
                P.w("vector", "act", a_rstd)
                P.w("vector", "pe", kvb_users_pe)
                P.wcur("vector", "sc")
                d_xn = P.dve("tensor_scalar", kvb[:, 0:4096], shf[:], stf[:, 2:3],
                             stf[:, 4:5], ALU.subtract, ALU.mult)
                x_users = (d_xn, d_xn)
                P.w("scalar", "dve", d_xn)
                c_xt = P.dma("scalar", xt_d[tc * 128:(tc + 1) * 128, :], kvb[:, 0:4096], "sc")
                P.setrel("xn_tok", "sc", c_xt)
                d_e = 0
                for fc in range(KC):
                    par = fc % 2
                    P.w("tensor", "dve", d_xn)
                    P.wrel("tensor", ("ps_tr", par))
                    p_t = P.tr(ps_tr2[par][:], kvb[:, fc * 128:(fc + 1) * 128], ident[:])
                    if fc == KC - 1:
                        kvb_users_pe = p_t
                    P.w("scalar", "pe", p_t)
                    d_e = P.act(big[:, fc * 2048 + tc * 128: fc * 2048 + tc * 128 + 128],
                                ps_tr2[par][:], AF.Identity)
                    P.setrel(("ps_tr", par), "act", d_e)
                P.w("tensor", "act", d_e)
                P.wrel("tensor", ("rb", 2))
                P.w("tensor", "sg", c_gwbf)
                P.mm(ps_e[:, 0:NE], onesr[0:1, :], gw_bf[0:1, KC * NE:(KC + 1) * NE],
                     True, False)
                p_g = None
                for kc in range(KC):
                    p_g = P.mm(ps_e[:, 0:NE],
                               big[:, kc * 2048 + tc * 128: kc * 2048 + tc * 128 + 128],
                               gw_bf[:, kc * NE:(kc + 1) * NE], False, kc == KC - 1,
                               inc=(kc == KC - 1))
                P.w("vector", "pe", p_g)
                d_gm = P.dve("reduce_max", stf[:, 7:8], ps_e[:, 0:NE], axis=AX.X)
                P.w("vector", "dve", d_gm)
                d_m = P.dve("tensor_scalar", masks[:, tc, :], ps_e[:, 0:NE],
                            stf[:, 7:8], 0.0, ALU.subtract, ALU.is_ge)
                P.setrel(("rb", 2), "dve", d_m)
                P.wrel("tensor", ("rb", 3))
                P.w("tensor", "dve", d_m)
                p_rk = None
                for tcp in range(tc + 1):
                    lhs = lt_sb[:, 0:128] if tcp == tc else lt_sb[:, 128:256]
                    p_rk = P.mm(ps_d[:, 0:NE], lhs, masks[:, tcp, :],
                                tcp == 0, tcp == tc, inc=(tcp == tc))
                P.w("vector", "pe", p_rk)
                ds1 = P.dve("tensor_single_scalar", slotf[:], ps_d[:, 0:NE],
                            float(CAP - 1), ALU.min)
                P.setrel(("rb", 3), "dve", ds1)
                P.w("vector", "dve", ds1)
                ds2 = P.dve("tensor_add", slotf[:], slotf[:], jb_sb[:])
                P.w("vector", "dve", ds2)
                ds3 = P.dve("tensor_mul", slotf[:], slotf[:], masks[:, tc, :])
                P.w("vector", "dve", ds3)
                ds4 = P.dve("reduce_sum", stf[:, 7:8], slotf[:], axis=AX.X)
                P.w("vector", "dve", ds4)
                ds5 = P.dve("tensor_copy", slotu[:, tc:tc + 1], stf[:, 7:8])
                P.w("vector", "dve", ds5)
                if tc > 0:
                    P.w("vector", "sg", c_sc)
                P.dve("tensor_scalar", slotw[:, 0:1], slotu[:, tc:tc + 1], 7, None,
                      ALU.logical_shift_right, inc=False)
                dw2 = P.dve("tensor_scalar", slotw[:, 1:2], slotu[:, tc:tc + 1], 127,
                            None, ALU.bitwise_and)
                P.w("vector", "dve", dw2)
                dw3 = P.dve("tensor_scalar", slotw[:, 1:2], slotw[:, 1:2], 5, None,
                            ALU.logical_shift_left)
                P.w("vector", "dve", dw3)
                dw4 = P.dve("tensor_tensor", slotw[:, 2:3], slotw[:, 0:1],
                            slotw[:, 1:2], ALU.bitwise_or)
                P.w("gpsimd", "dve", dw4)
                P.w("gpsimd", "sx", prelude_sx)
                c_sc = P.idma(gl_d, bass.IndirectOffsetOnAxis(slotw[:, 2:3], 0),
                              iota_sb[:, tc:tc + 1], None)
                if tc % 4 == 3:
                    trc = tc // 4
                    P.w("scalar", "act", d_e)
                    c_xnT = P.dma(
                        "scalar",
                        xnT_d.rearrange("(c p) t -> p c t", p=128)[:, :, trc * 512:(trc + 1) * 512],
                        big[:].rearrange("p (c t) -> p c t", t=2048)[:, :, trc * 512:(trc + 1) * 512],
                        "sc")

            stage = (smb[:, 512:1024], smb[:, 1024:1536])
            nst = 0
            p_last = 0
            d_e_final = P.c["act"]
            for hq in range(4):
                half = hq % 2
                base = half * 8192
                cols = 4096 + hq * 256
                P.wrel("gpsimd", ("auxh", half))
                c_wkq = P.dma("gpsimd",
                              aux[:, base:base + 8192].rearrange("p (c n) -> p c n", n=256),
                              wqkv_d[:, cols:cols + 256].rearrange("(c p) n -> p c n", p=128),
                              "sg")
                for hh in range(2):
                    h = hq * 2 + hh
                    for trng in range(NTR):
                        par = nst % 2
                        P.w("tensor", "sg", c_wkq)
                        P.w("tensor", "act", d_e_final)
                        P.wrel("tensor", ("rb", par))
                        p_last = None
                        for kc in range(KC):
                            p_last = P.mm(
                                ps_ab[par][:],
                                aux[:, base + kc * 256 + hh * 128: base + kc * 256 + hh * 128 + 128],
                                big[:, kc * 2048 + trng * 512: kc * 2048 + (trng + 1) * 512],
                                kc == 0, kc == KC - 1, inc=(kc == KC - 1))
                        spar = nst % 2
                        nst += 1
                        P.w("vector", "pe", p_last)
                        P.wcur("vector", "sx")
                        d_k = P.dve("tensor_copy", stage[spar], ps_ab[par][:])
                        P.setrel(("rb", par), "dve", d_k)
                        P.w("sync", "dve", d_k)
                        c_st = P.dma("sync",
                                     kT_d[h * 128:(h + 1) * 128, trng * 512:(trng + 1) * 512],
                                     stage[spar], "sx")
                        P.setrel(("stage", spar), "sx", c_st)
                P.setrel(("auxh", half), "pe", p_last)
            for q4 in range(2, 4):
                P.wrel("gpsimd", ("auxh", 0))
                P.wrel("gpsimd", ("auxh", 1))
                c_wkq = P.dma("gpsimd", aux[:].rearrange("p (c n) -> p c n", n=512),
                              wqkv_d[:, 4096 + q4 * 512:4096 + (q4 + 1) * 512]
                              .rearrange("(c p) n -> p c n", p=128),
                              "sg")
                for tc in range(NTC):
                    par = nst % 2
                    P.w("tensor", "sg", c_wkq)
                    P.w("tensor", "act", d_e_final)
                    P.wrel("tensor", ("rb", par))
                    p_last = None
                    for kc in range(KC):
                        p_last = P.mm(
                            ps_ab[par][:],
                            big[:, kc * 2048 + tc * 128: kc * 2048 + tc * 128 + 128],
                            aux[:, kc * 512:(kc + 1) * 512],
                            kc == 0, kc == KC - 1, inc=(kc == KC - 1))
                    spar = nst % 2
                    nst += 1
                    P.w("vector", "pe", p_last)
                    P.wcur("vector", "sx")
                    d_v = P.dve("tensor_copy", stage[spar], ps_ab[par][:])
                    P.setrel(("rb", par), "dve", d_v)
                    P.w("sync", "dve", d_v)
                    c_sv = P.dma("sync",
                                 v_d[tc * 128:(tc + 1) * 128, (q4 - 2) * 512:(q4 - 1) * 512],
                                 stage[spar], "sx")
                    P.setrel(("stage", spar), "sx", c_sv)
                P.setrel(("auxh", 0), "pe", p_last)
                P.setrel(("auxh", 1), "pe", p_last)
            for g in range(GQ):
                P.setrel(("wq", g), "pe", p_last)

            P.setrel(("wo", 0), "sc", c_xnT)
            P.setrel(("wo", 1), "sc", c_xnT)
            P.setrel("otn_region", "sc", c_xnT)
            if DEBUG:
                P.w("sync", "dve", P.c["dve"])
                P.dma("sync", masks_d, masks[:], "out")

            if PHASES < 2:
                fin(P)
                return

            def xsl(kc):
                return big[:, kc * 512:(kc + 1) * 512]

            def otn(h, lo, w):
                return big[:, 16384 + h * 512 + lo: 16384 + h * 512 + lo + w]

            def wosl(wpar, kc):
                o = 32768 + wpar * 16384 + kc * 512
                return big[:, o:o + 512]

            qT = smb[:, 0:512]
            pt = (smb[:, 512:1024], smb[:, 1024:1536])
            ex = (smb[:, 1536:2048], smb[:, 2048:2560])
            recb = smb[:, 2560:3072]
            sst = (smb[:, 3072:3584], smb[:, 3584:4096])
            kT_sb = kvb[:, 0:2048]
            sqd_sb = shf[:, 0:MCOL]
            recf = shf[0:1, 3584:4096]

            P.w("sync", "dve", x_users[0])
            c_sqd = P.dma("sync", sqd_sb, sqd_d, "sx")

            first_otn = True
            p_q = 0
            for trng in range(NTR):
                KT = 4 * (trng + 1)
                kend = 512 * (trng + 1)
                if trng == 0:
                    P.w("sync", "sc", c_xnT)
                else:
                    P.wrel("sync", "xnT_sl")
                c_xsl = P.dma("sync",
                              big[:, 0:16384].rearrange("p (c t) -> p c t", t=512),
                              xnT_d.rearrange("(c p) t -> p c t", p=128)[:, :, trng * 512:(trng + 1) * 512],
                              "sx")
                for h in range(NKV):
                    P.wrel("sync", "kT_sb")
                    c_kT = P.dma("sync", kT_sb[:, 0:kend],
                                 kT_d[h * 128:(h + 1) * 128, 0:kend], "sx")
                    P.wrel("sync", "v_sb")
                    c_v = P.dma("sync",
                                kvb[:, 2048:2048 + 128 * KT].rearrange("p (c d) -> p c d", d=128),
                                v_d[0:kend, h * 128:(h + 1) * 128].rearrange("(c p) d -> p c d", p=128),
                                "sx")
                    c_wq = {}
                    for g in range(GQ):
                        P.wrel("gpsimd", ("wq", g))
                        c_wq[g] = P.dma(
                            "gpsimd",
                            aux[:].rearrange("p (c n) -> p c n", n=512)[:, :, g * 128:(g + 1) * 128],
                            wqkv_d[:, (h * GQ + g) * 128:(h * GQ + g + 1) * 128]
                            .rearrange("(c p) n -> p c n", p=128),
                            "sg")
                    psc = {}
                    pvv = {}
                    if P2_STAGE == "loads":
                        P.setrel("kT_sb", "sx", c_v)
                        P.setrel("v_sb", "sx", c_v)
                        continue
                    for g in range(GQ):
                        head = h * GQ + g
                        slope = float(2.0 ** (-0.25 * (head + 1)))
                        P.wrel("scalar", "metb")
                        P.w("scalar", "sx", c_sqd)
                        a_met = P.act(metb[:], sqd_sb, AF.Exp, scale=-slope)
                        P.wrel("tensor", ("rb", 2))
                        P.w("tensor", "sg", c_wq[g])
                        P.w("tensor", "sx", c_xsl)
                        for kc in range(KC):
                            p_q = P.mm(ps_e[:],
                                       aux[:, kc * 512 + g * 128: kc * 512 + g * 128 + 128],
                                       xsl(kc), kc == 0, kc == KC - 1, inc=(kc == KC - 1))
                        P.w("vector", "pe", p_q)
                        P.wrel("vector", "qT")
                        d_q = P.dve("tensor_copy", qT, ps_e[:])
                        P.setrel(("rb", 2), "dve", d_q)
                        P.setrel(("wq", g), "pe", p_q)
                        P.setrel("qT", "pe", p_q)
                        P.setrel("metb", "dve", d_q)
                        if P2_STAGE == "qgen":
                            continue

                        exv = {}
                        ptv = {}
                        psc = {}
                        pvv = {}

                        ps_sc = (ps_a, ps_b, ps_e, ps_d)

                        def em_score(kt):
                            b4 = kt % 4
                            P.w("tensor", "act", exv.get(kt - 4, 0))
                            P.wrel("tensor", ("rb", b4))
                            if kt == 0:
                                P.w("tensor", "dve", d_q)
                                P.w("tensor", "sx", c_kT)
                            return P.mm(ps_sc[b4][:], kT_sb[:, kt * 128:(kt + 1) * 128],
                                        qT, True, True, inc=True)

                        for kt0 in range(min(4, KT)):
                            psc[kt0] = em_score(kt0)
                        p_den = None
                        for kt in range(KT):
                            b4 = kt % 4
                            par = kt % 2
                            P.w("scalar", "pe", psc[kt])
                            P.w("scalar", "dve", ptv.get(kt - 2, 0))
                            exv[kt] = P.act(ex[par], ps_sc[b4][:], AF.Exp, scale=SCALE)
                            P.setrel(("rb", b4), "act", exv[kt])
                            m0 = trng * 512 - kt * 128 + 384
                            P.w("vector", "act", exv[kt])
                            P.w("vector", "pe", pvv.get(kt - 2, 0))
                            if kt == 0:
                                P.w("vector", "act", a_met)
                            ptv[kt] = P.dve("tensor_mul", pt[par], ex[par],
                                            metb[:, m0:m0 + 512])
                            if kt + 4 < KT:
                                psc[kt + 4] = em_score(kt + 4)
                            P.w("tensor", "dve", ptv[kt])
                            if kt == 0:
                                P.wrel("tensor", "ps_den")
                                P.wrel("tensor", "ps_c")
                                P.w("tensor", "sx", c_v)
                            P.mm(ps_den[:], onesc[:], pt[par], kt == 0, kt == KT - 1,
                                 inc=(kt == KT - 1))
                            if kt == KT - 1:
                                p_den = P.c["pe"]
                            pvv[kt] = P.mm(ps_c[:],
                                           kvb[:, 2048 + kt * 128:2048 + (kt + 1) * 128],
                                           pt[par], kt == 0, kt == KT - 1, inc=True)
                        P.setrel("metb", "dve", ptv[KT - 1])
                        P.setrel("qT", "pe", psc[KT - 1])
                        P.w("vector", "pe", p_den)
                        d_rc = P.dve("reciprocal", recf[:], ps_den[0:1, :])
                        P.w("vector", "dve", d_rc)
                        d_rec = P.dve("tensor_copy", recbf[:], recf[:])
                        P.setrel("ps_den", "dve", d_rec)
                        P.wrel("tensor", ("rb", 3))
                        P.w("tensor", "dve", d_rec)
                        p_bc = P.mm(ps_d[:], onesr[0:1, :], recbf[0:1, :], True, True,
                                    inc=True)
                        P.w("vector", "pe", p_bc)
                        d_recb = P.dve("tensor_copy", recb, ps_d[:])
                        P.setrel(("rb", 3), "dve", d_recb)
                        if first_otn:
                            P.wcur("vector", "sc")
                            first_otn = False
                        P.w("vector", "dve", d_recb)
                        d_otn = P.dve("tensor_mul", otn(head, 0, 512), ps_c[:], recb)
                        P.setrel("ps_c", "dve", d_otn)
                    if psc:
                        P.setrel("kT_sb", "pe", psc[KT - 1])
                        P.setrel("v_sb", "pe", pvv[KT - 1])
                    elif P2_STAGE == "qgen":
                        P.setrel("kT_sb", "sx", c_v)
                        P.setrel("v_sb", "sx", c_v)
                if P2_STAGE != "full":
                    P.setrel("xnT_sl", "pe", p_q)
                    continue
                d_otn_all = P.c["dve"]
                p_s = 0
                for fo in range(NOR):
                    wpar = fo % 2
                    if trng == 0 and fo < 2:
                        P.wcur("gpsimd", "sc")
                    P.wrel("gpsimd", ("wo", wpar))
                    c_wo = P.dma("gpsimd",
                                 big[:, 32768 + wpar * 16384:32768 + (wpar + 1) * 16384]
                                 .rearrange("p (c n) -> p c n", n=512),
                                 wo_d[:, fo * 512:(fo + 1) * 512].rearrange("(c p) n -> p c n", p=128),
                                 "sg")
                    for tt in range(4):
                        par = tt % 2
                        P.w("tensor", "sg", c_wo)
                        if fo == 0 and tt == 0:
                            P.w("tensor", "dve", d_otn_all)
                        P.wrel("tensor", ("rb", par))
                        for hc in range(KC):
                            p_s = P.mm(ps_ab[par][:], otn(hc, tt * 128, 128), wosl(wpar, hc),
                                       hc == 0, hc == KC - 1, inc=(hc == KC - 1))
                        if tt == 3:
                            P.setrel(("wo", wpar), "pe", p_s)
                        P.w("vector", "pe", p_s)
                        P.wcur("vector", "sc")
                        d_s = P.dve("tensor_copy", sst[par], ps_ab[par][:])
                        P.setrel(("rb", par), "dve", d_s)
                        P.w("scalar", "dve", d_s)
                        c_ss = P.dma("scalar",
                                     sT_d[trng * 512 + tt * 128: trng * 512 + (tt + 1) * 128,
                                          fo * 512:(fo + 1) * 512],
                                     sst[par], "sc")
                        P.setrel(("sst", par), "sc", c_ss)
                P.setrel("xnT_sl", "pe", p_q)

            c_sT_done = P.c["sx"]

            if PHASES < 4:
                fin(P)
                return

            def wjsl(wpar, kc):
                o = 24576 + wpar * 16896 + kc * 512
                return big[:, o:o + 512]

            def xgt(gpar):
                return big[:, gpar * 4096:(gpar + 1) * 4096]

            def xntg(kc, lo, w):
                return big[:, 8192 + kc * 512 + lo: 8192 + kc * 512 + lo + w]

            ps_rot = (ps_a, ps_b, ps_e)
            for rpar, src in ((0, ("rb", 0)), (1, ("rb", 1)), (2, ("rb", 2))):
                if src in P.rel:
                    P.rel[("ps_rot", rpar)] = P.rel[src]
            last_wo_pe = P.c["pe"]
            last_p23_dve = P.c["dve"]
            P.setrel(("wj", 0), "pe", last_wo_pe)
            P.setrel(("wj", 1), "pe", last_wo_pe)
            nrot = 0
            nst4 = 0
            NTILE = CAP // 128

            P.w("sync", "sg", c_sc)
            c_gl = P.dma("sync", glall[:],
                         gl_d.rearrange("(p c) o -> p c o", c=32), "sx")
            for j in range(NE):
                d_xnTg = {}
                p_t = 0
                for tile in range(NTILE):
                    gpar = tile % 2
                    P.wrel("gpsimd", ("xgt", gpar))
                    P.w("gpsimd", "sx", c_gl)
                    P.wcur("gpsimd", "sc")
                    if j == 0 and tile == 0:
                        P.w("gpsimd", "pe", last_wo_pe)
                        P.w("gpsimd", "dve", last_p23_dve)
                    c_xg = P.idma(xgt(gpar), None, xt_d,
                                  bass.IndirectOffsetOnAxis(glall[:, j * 4 + tile, :], 0))
                    d_tx = None
                    for fc in range(KC):
                        tpar = fc % 2
                        if fc == 0:
                            P.w("tensor", "sg", c_xg)
                        P.wrel("tensor", ("ps_tr", tpar))
                        p_t = P.tr(ps_tr2[tpar][:], xgt(gpar)[:, fc * 128:(fc + 1) * 128],
                                   ident[:])
                        P.w("vector", "pe", p_t)
                        d_tx = P.dve("tensor_copy", xntg(fc, tile * 128, 128),
                                     ps_tr2[tpar][:])
                        P.setrel(("ps_tr", tpar), "dve", d_tx)
                    P.setrel(("xgt", gpar), "pe", p_t)
                    d_xnTg[tile] = d_tx
                for o in range(NOR):
                    wpar = (j * NOR + o) % 2
                    P.wrel("gpsimd", ("wj", wpar))
                    c_wj = P.dma("gpsimd",
                                 big[:, 24576 + wpar * 16896:24576 + wpar * 16896 + 16896]
                                 .rearrange("p (c n) -> p c n", n=512),
                                 wjT_d[j][:, o * 512:(o + 1) * 512]
                                 .rearrange("(c p) n -> p c n", p=128),
                                 "sg")
                    for tile in range(NTILE):
                        rpar = nrot % 3
                        nrot += 1
                        P.w("tensor", "sg", c_wj)
                        if o == 0:
                            P.w("tensor", "dve", d_xnTg[tile])
                        P.wrel("tensor", ("ps_rot", rpar))
                        P.mm(ps_rot[rpar][:], onesr[0:1, :], wjsl(wpar, KC)[0:1, :],
                             True, False)
                        p_x = None
                        for kc in range(KC):
                            p_x = P.mm(ps_rot[rpar][:], xntg(kc, tile * 128, 128),
                                       wjsl(wpar, kc), False, kc == KC - 1,
                                       inc=(kc == KC - 1))
                        if tile == NTILE - 1:
                            P.setrel(("wj", wpar), "pe", p_x)
                        spar = nst4 % 2
                        nst4 += 1
                        P.w("vector", "pe", p_x)
                        P.wcur("vector", "sc")
                        d_s = P.dve("tensor_copy", sst[spar], ps_rot[rpar][:])
                        P.setrel(("ps_rot", rpar), "dve", d_s)
                        P.w("scalar", "dve", d_s)
                        c_f = P.dma("scalar",
                                    ffn_d[j * CAP + tile * 128: j * CAP + (tile + 1) * 128,
                                          o * 512:(o + 1) * 512],
                                    sst[spar], "sc")
                        P.setrel(("sst", spar), "sc", c_f)
            bigf = big[:].bitcast(F32)

            def xrow(par):
                return bigf[:, 8192 + par * 4096: 8192 + (par + 1) * 4096]

            g4d = (metb[:, 0:512], metb[:, 512:1024])
            c_ffn_done = P.c["sc"]
            a_sqd_users2 = P.c["act"]
            prev_g4 = {}
            for tc in range(NTC):
                fpar = tc % 2
                P.wrel("gpsimd", ("ft", fpar))
                P.w("gpsimd", "sc", c_ffn_done)
                c_fg = P.idma(xgt(fpar), None, ffn_d,
                              bass.IndirectOffsetOnAxis(slotu[:, tc:tc + 1], 0))
                P.wrel("sync", ("stok", fpar))
                if tc < 2:
                    P.w("sync", "pe", P.c["pe"])
                    P.w("sync", "sc", c_ffn_done)
                c_s = P.dma("sync", big[:, 8192 + fpar * 4096:8192 + (fpar + 1) * 4096],
                            sT_d[tc * 128:(tc + 1) * 128, :], "sx")
                P.wcur("sync", "out")
                c_x = P.dma("sync", xrow(fpar), x_d[tc * 128:(tc + 1) * 128, :], "sx")
                d_out = 0
                a_g = 0
                for o in range(NOR):
                    gp4 = o % 2
                    P.w("scalar", "sg", c_fg)
                    P.w("scalar", "dve", prev_g4.get(gp4, 0))
                    a_g = P.act(g4d[gp4], xgt(fpar)[:, o * 512:(o + 1) * 512], AF.Silu)
                    P.w("vector", "act", a_g)
                    P.w("vector", "sx", c_s)
                    d_sg = P.dve("tensor_mul", out4[:], g4d[gp4],
                                 big[:, 8192 + fpar * 4096 + o * 512:
                                     8192 + fpar * 4096 + (o + 1) * 512])
                    prev_g4[gp4] = d_sg
                    if o == 0:
                        P.w("vector", "sx", c_x)
                    P.w("vector", "dve", d_sg)
                    d_out = P.dve("tensor_add", xrow(fpar)[:, o * 512:(o + 1) * 512],
                                  xrow(fpar)[:, o * 512:(o + 1) * 512], out4[:])
                P.w("scalar", "dve", d_out)
                c_y = P.dma("scalar", y_d[tc * 128:(tc + 1) * 128, :], xrow(fpar), "out")
                P.setrel(("xrow", fpar), "out", c_y)
                P.setrel(("ft", fpar), "act", a_g)
                P.setrel(("stok", fpar), "dve", d_sg)
            fin(P)

        @block.tensor
        def _(eng):
            body(Em(nc, "tensor", eng, sems))

        @block.vector
        def _(eng):
            body(Em(nc, "vector", eng, sems))

        @block.scalar
        def _(eng):
            body(Em(nc, "scalar", eng, sems))

        @block.gpsimd
        def _(eng):
            body(Em(nc, "gpsimd", eng, sems))

        @block.sync
        def _(eng):
            body(Em(nc, "sync", eng, sems))

    return nc


def prep_inputs(L0_attn_hidden_states, L0_attn_w_qkv, L0_attn_w_o,
                L0_ffn_gate_w, L0_ffn_gate_b, L0_ffn_expert_w, L0_ffn_expert_b):
    x = np.asarray(L0_attn_hidden_states, dtype=np.float32)
    w_qkv = np.ascontiguousarray(np.asarray(L0_attn_w_qkv, dtype=np.float32))
    w_o = np.ascontiguousarray(np.asarray(L0_attn_w_o, dtype=np.float32))
    gw = np.asarray(L0_ffn_gate_w, dtype=np.float32)
    gwT = gw.T.reshape(KC, 128, NE).transpose(1, 0, 2)
    gbpad = np.zeros((128, 1, NE), dtype=np.float32)
    gbpad[0, 0, :] = np.asarray(L0_ffn_gate_b, dtype=np.float32)
    gwT = np.ascontiguousarray(np.concatenate([gwT, gbpad], axis=1))
    wjT = np.asarray(L0_ffn_expert_w, dtype=np.float32).transpose(0, 2, 1)
    ebpad = np.zeros((NE, 128, DM), dtype=np.float32)
    ebpad[:, 0, :] = np.asarray(L0_ffn_expert_b, dtype=np.float32)
    wjT = np.ascontiguousarray(np.concatenate([wjT, ebpad], axis=1))

    lte = (np.arange(128)[:, None] < np.arange(128)[None, :])
    ltones = np.concatenate([lte, np.ones((128, 128))], axis=1).astype(ml_dtypes.bfloat16)
    jb = np.broadcast_to((np.arange(NE) * CAP).astype(np.float32), (128, NE)).copy()
    iota16 = (np.arange(NTC)[None, :] * 128 + np.arange(128)[:, None]).astype(np.uint32)

    ki = np.arange(128)[:, None]
    m = np.arange(MCOL)[None, :]
    d = m - 384 - ki
    sqd = np.where(d >= 0, np.sqrt(np.maximum(d, 0).astype(np.float64)), 1e6).astype(np.float32)

    ident = np.eye(128).astype(ml_dtypes.bfloat16)
    ones_row = np.ones((1, 128), dtype=ml_dtypes.bfloat16)
    ones_col = np.ones((128, 1), dtype=ml_dtypes.bfloat16)

    shared = {"w_qkv": w_qkv, "w_o": w_o, "gwT": gwT,
              "wjT": wjT, "sqd": sqd, "ident_i": ident,
              "ones_row_i": ones_row, "ones_col_i": ones_col,
              "ltones": ltones, "jb": jb, "iota16": iota16}
    in_maps = []
    for b in range(8):
        im = dict(shared)
        im["x"] = np.ascontiguousarray(x[b])
        in_maps.append(im)
    return in_maps


def kernel(**inputs):
    in_maps = prep_inputs(**inputs)
    nc = build_nc()
    res = run_bass_kernel_spmd(nc, in_maps, core_ids=list(range(8)))
    out = np.stack([np.asarray(r["y"]) for r in res.results], axis=0)
    return out


# version 13
# speedup vs baseline: 1.0884x; 1.0884x over previous
import numpy as np
import ml_dtypes
from contextlib import ExitStack

import concourse.bass as bass
import concourse.mybir as mybir
from concourse.bass_utils import run_bass_kernel_spmd

F32 = mybir.dt.float32
BF16 = mybir.dt.bfloat16
AF = mybir.ActivationFunctionType
ALU = mybir.AluOpType
AX = mybir.AxisListType

T = 2048
DM = 4096
KC = 32
NTC = 16
NTR = 4
NKV = 8
GQ = 4
NE = 8
NOR = 8
SCALE = float(128.0 ** -0.5)
EPS = 1e-5
MCOL = 2432

CAP = 512
NSLOT = NE * CAP

PHASES = 4
DEBUG = False
P2_STAGE = "full"


class Em:

    def __init__(self, nc, role, eng, sem):
        self.nc = nc
        self.role = role
        self.eng = eng
        self.sem = sem
        self.c = dict(pe=0, act=0, dve=0, sx=0, sc=0, sg=0, out=0,
                      sta=0, stb=0, ssa=0, ssb=0, ya=0, yb=0, sxt=0)
        self.rel = {}

    def w(self, role, sname, val):
        if val is None or val <= 0:
            return
        if self.role == role:
            self.eng.wait_ge(self.sem[sname], val)

    def wcur(self, role, sname):
        self.w(role, sname, self.c[sname])

    def wrel(self, role, key):
        r = self.rel.get(key)
        if r is not None:
            self.w(role, r[0], r[1])

    def setrel(self, key, sname, val):
        self.rel[key] = (sname, val)

    def _inc(self, inst, sname, n):
        if sname is None:
            return None
        self.c[sname] += n
        if inst is not None:
            inst.then_inc(self.sem[sname], n)
        return self.c[sname]

    def mm(self, out, lhsT, rhs, start, stop, inc=False):
        inst = None
        if self.role == "tensor":
            inst = self.nc.tensor.matmul(out, lhsT, rhs, start=start, stop=stop,
                                         skip_group_check=True)
        return self._inc(inst, "pe" if inc else None, 1)

    def tr(self, out, in_, ident, inc=True):
        inst = None
        if self.role == "tensor":
            inst = self.nc.tensor.transpose(out, in_, ident)
        return self._inc(inst, "pe" if inc else None, 1)

    def act(self, out, in_, func, bias=0.0, scale=1.0, accum_out=None, inc=True):
        inst = None
        if self.role == "scalar":
            inst = self.nc.scalar.activation(out, in_, func, bias=bias, scale=scale,
                                             accum_out=accum_out)
        return self._inc(inst, "act" if inc else None, 1)

    def dve(self, method, *args, inc=True, **kw):
        inst = None
        if self.role == "vector":
            inst = getattr(self.nc.vector, method)(*args, **kw)
        return self._inc(inst, "dve" if inc else None, 1)

    def dma(self, qrole, out, in_, sname):
        inst = None
        if self.role == qrole:
            eng = {"sync": self.nc.sync, "scalar": self.nc.scalar,
                   "gpsimd": self.nc.gpsimd}[qrole]
            inst = eng.dma_start(out=out, in_=in_)
        return self._inc(inst, sname, 16)

    def idma(self, out, out_off, in_, in_off):
        inst = None
        if self.role == "gpsimd":
            inst = self.nc.gpsimd.indirect_dma_start(out, out_off, in_, in_off)
        return self._inc(inst, "sg", 16)


def build_nc():
    nc = bass.Bass("TRN2", target_bir_lowering=False)

    x_d = nc.dram_tensor("x", [T, DM], F32, kind="ExternalInput").ap()
    wqkv_d = nc.dram_tensor("w_qkv", [DM, 6144], F32, kind="ExternalInput").ap()
    wo_d = nc.dram_tensor("w_o", [DM, DM], F32, kind="ExternalInput").ap()
    gwT_d = nc.dram_tensor("gwT", [128, KC + 1, NE], F32, kind="ExternalInput").ap()
    wjT_d = nc.dram_tensor("wjT", [NE, DM + 128, DM], F32, kind="ExternalInput").ap()
    sqd_d = nc.dram_tensor("sqd", [128, MCOL], F32, kind="ExternalInput").ap()
    lt_d = nc.dram_tensor("ltones", [128, 256], BF16, kind="ExternalInput").ap()
    jb_d = nc.dram_tensor("jb", [128, NE], F32, kind="ExternalInput").ap()
    iota_d = nc.dram_tensor("iota16", [128, NTC], mybir.dt.uint32, kind="ExternalInput").ap()
    ident_d = nc.dram_tensor("ident_i", [128, 128], BF16, kind="ExternalInput").ap()
    onesr_d = nc.dram_tensor("ones_row_i", [1, 128], BF16, kind="ExternalInput").ap()
    onesc_d = nc.dram_tensor("ones_col_i", [128, 1], BF16, kind="ExternalInput").ap()

    y_d = nc.dram_tensor("y", [T, DM], F32, kind="ExternalOutput").ap()
    ikind = "ExternalOutput" if DEBUG else "Internal"
    xnT_d = nc.dram_tensor("xnT_s", [DM, T], BF16, kind=ikind).ap()
    kT_d = nc.dram_tensor("kT_s", [NKV * 128, T], BF16, kind=ikind).ap()
    v_d = nc.dram_tensor("v_s", [T, NKV * 128], BF16, kind=ikind).ap()
    sT_d = nc.dram_tensor("sT_s", [T, DM], BF16, kind=ikind).ap()
    xt_d = nc.dram_tensor("xtok_s", [T, DM], BF16, kind="Internal").ap()
    ffn_d = nc.dram_tensor("ffn_s", [NSLOT, DM], BF16, kind=ikind).ap()
    gl_d = nc.dram_tensor("glist_s", [NSLOT, 1], mybir.dt.uint32, kind=ikind).ap()
    masks_d = (nc.dram_tensor("masks_s", [128, NTC, NE], BF16, kind="ExternalOutput").ap()
               if DEBUG else None)

    with ExitStack() as st:
        E = st.enter_context
        big = E(nc.sbuf_tensor("big", [128, 65536], BF16))
        aux = E(nc.sbuf_tensor("aux", [128, 16384], BF16))
        kvb = E(nc.sbuf_tensor("kvb", [128, 4096], BF16))
        shf = E(nc.sbuf_tensor("shf", [128, 4096], F32))
        metb = E(nc.sbuf_tensor("metb", [128, MCOL], BF16))
        smb = E(nc.sbuf_tensor("smb", [128, 4096], BF16))
        stf = E(nc.sbuf_tensor("stf", [128, 8], F32))
        masks = E(nc.sbuf_tensor("masks", [128, NTC, NE], BF16))
        lt_sb = E(nc.sbuf_tensor("lt_sb", [128, 256], BF16))
        jb_sb = E(nc.sbuf_tensor("jb_sb", [128, NE], F32))
        slotf = E(nc.sbuf_tensor("slotf", [128, NE], F32))
        slotu = E(nc.sbuf_tensor("slotu", [128, NTC], mybir.dt.uint32))
        iota_sb = E(nc.sbuf_tensor("iota_sb", [128, NTC], mybir.dt.uint32))
        zsb = E(nc.sbuf_tensor("zsb", [128, 32], mybir.dt.uint32))
        slotw = E(nc.sbuf_tensor("slotw", [128, 4], mybir.dt.uint32))
        glall = E(nc.sbuf_tensor("glall", [128, 32, 1], mybir.dt.uint32))
        recbf = E(nc.sbuf_tensor("recbf", [1, 512], BF16))
        out4 = E(nc.sbuf_tensor("out4", [128, 512], F32))
        gw_bf = E(nc.sbuf_tensor("gw_bf", [128, (KC + 1) * NE], BF16))
        ident = E(nc.sbuf_tensor("ident", [128, 128], BF16))
        onesr = E(nc.sbuf_tensor("onesr", [1, 128], BF16))
        onesc = E(nc.sbuf_tensor("onesc", [128, 1], BF16))

        ps_a = E(nc.psum_tensor("ps_a", [128, 512], F32))
        ps_b = E(nc.psum_tensor("ps_b", [128, 512], F32))
        ps_c = E(nc.psum_tensor("ps_c", [128, 512], F32))
        ps_d = E(nc.psum_tensor("ps_d", [128, 512], F32))
        ps_e = E(nc.psum_tensor("ps_e", [128, 512], F32))
        ps_den = E(nc.psum_tensor("ps_den", [1, 512], F32))
        ps_t0 = E(nc.psum_tensor("ps_t0", [128, 128], BF16))
        ps_t1 = E(nc.psum_tensor("ps_t1", [128, 128], BF16))

        sems = {}
        for sn in ("pe", "act", "dve", "sx", "sc", "sg", "out",
                   "sta", "stb", "ssa", "ssb", "ya", "yb", "sxt"):
            sems[sn] = E(nc.semaphore(sn))
        block = E(nc.Block())

        def fin(P: Em):
            for sn in ("out", "sta", "stb", "ssa", "ssb", "ya", "yb", "sxt"):
                P.w("sync", sn, P.c[sn])
            P.w("sync", "sx", P.c["sx"])
            P.w("sync", "sc", P.c["sc"])
            P.w("sync", "sg", P.c["sg"])
            P.w("sync", "pe", P.c["pe"])
            P.w("sync", "dve", P.c["dve"])
            P.w("sync", "act", P.c["act"])

        def body(P: Em):
            ps_ab = (ps_a, ps_b)
            ps_tr2 = (ps_t0, ps_t1)

            P.dma("sync", ident[:], ident_d, "sx")
            c_onesr = P.dma("sync", onesr[:], onesr_d, "sx")
            P.dma("sync", onesc[:], onesc_d, "sx")
            c_gwbf = P.dma("gpsimd", gw_bf[:], gwT_d.rearrange("p c e -> p (c e)"), "sg")
            P.dma("sync", lt_sb[:], lt_d, "sx")
            P.dma("sync", jb_sb[:], jb_d, "sx")
            c_iota = P.dma("sync", iota_sb[:], iota_d, "sx")
            d_z = P.dve("memset", zsb[:], 0)
            P.w("sync", "dve", d_z)
            c_glz = P.dma("sync", gl_d.rearrange("(p c) o -> p (c o)", c=32),
                          zsb[:], "sx")

            x_users = None
            kvb_users_pe = 0
            a_sq = 0

            for tc in range(NTC):
                if x_users is not None:
                    P.w("sync", "dve", x_users[0])
                c_x = P.dma("sync", shf[:], x_d[tc * 128:(tc + 1) * 128, :], "sx")
                if tc == 0:
                    prelude_sx = c_x
                P.w("vector", "sx", c_x)
                d1 = P.dve("reduce_sum", stf[:, 0:1], shf[:], axis=AX.X)
                P.w("scalar", "sx", c_x)
                a_sq = P.act(aux[:, 0:4096], shf[:], AF.Square, accum_out=stf[:, 1:2])
                P.setrel("aux", "act", a_sq)
                P.w("vector", "act", a_sq)
                P.w("vector", "dve", d1)
                d2 = P.dve("tensor_scalar_mul", stf[:, 2:3], stf[:, 0:1], 1.0 / DM)
                P.dve("tensor_scalar_mul", stf[:, 3:4], stf[:, 1:2], 1.0 / DM, inc=False)
                P.w("vector", "dve", d2)
                d4 = P.dve("tensor_mul", stf[:, 5:6], stf[:, 2:3], stf[:, 2:3])
                P.w("vector", "dve", d4)
                d5 = P.dve("tensor_sub", stf[:, 3:4], stf[:, 3:4], stf[:, 5:6])
                P.w("vector", "dve", d5)
                d_var = P.dve("tensor_scalar_add", stf[:, 3:4], stf[:, 3:4], EPS)
                P.w("scalar", "dve", d_var)
                a_ln = P.act(stf[:, 5:6], stf[:, 3:4], AF.Ln)
                P.w("scalar", "act", a_ln)
                a_rstd = P.act(stf[:, 4:5], stf[:, 5:6], AF.Exp, scale=-0.5)
                P.w("vector", "act", a_rstd)
                P.w("vector", "pe", kvb_users_pe)
                P.wcur("vector", "sxt")
                d_xn = P.dve("tensor_scalar", kvb[:, 0:4096], shf[:], stf[:, 2:3],
                             stf[:, 4:5], ALU.subtract, ALU.mult)
                x_users = (d_xn, d_xn)
                P.w("scalar", "dve", d_xn)
                c_xt = P.dma("scalar", xt_d[tc * 128:(tc + 1) * 128, :], kvb[:, 0:4096], "sxt")
                d_e = 0
                for fc in range(KC):
                    par = fc % 2
                    P.w("tensor", "dve", d_xn)
                    P.wrel("tensor", ("ps_tr", par))
                    p_t = P.tr(ps_tr2[par][:], kvb[:, fc * 128:(fc + 1) * 128], ident[:])
                    if fc == KC - 1:
                        kvb_users_pe = p_t
                    P.w("scalar", "pe", p_t)
                    d_e = P.act(big[:, fc * 2048 + tc * 128: fc * 2048 + tc * 128 + 128],
                                ps_tr2[par][:], AF.Identity)
                    P.setrel(("ps_tr", par), "act", d_e)
                P.w("tensor", "act", d_e)
                P.wrel("tensor", ("rb", 2))
                P.w("tensor", "sg", c_gwbf)
                P.mm(ps_e[:, 0:NE], onesr[0:1, :], gw_bf[0:1, KC * NE:(KC + 1) * NE],
                     True, False)
                p_g = None
                for kc in range(KC):
                    p_g = P.mm(ps_e[:, 0:NE],
                               big[:, kc * 2048 + tc * 128: kc * 2048 + tc * 128 + 128],
                               gw_bf[:, kc * NE:(kc + 1) * NE], False, kc == KC - 1,
                               inc=(kc == KC - 1))
                P.w("vector", "pe", p_g)
                d_gm = P.dve("reduce_max", stf[:, 7:8], ps_e[:, 0:NE], axis=AX.X)
                P.w("vector", "dve", d_gm)
                d_m = P.dve("tensor_scalar", masks[:, tc, :], ps_e[:, 0:NE],
                            stf[:, 7:8], 0.0, ALU.subtract, ALU.is_ge)
                P.setrel(("rb", 2), "dve", d_m)
                P.wrel("tensor", ("rb", 3))
                P.w("tensor", "dve", d_m)
                p_rk = None
                for tcp in range(tc + 1):
                    lhs = lt_sb[:, 0:128] if tcp == tc else lt_sb[:, 128:256]
                    p_rk = P.mm(ps_d[:, 0:NE], lhs, masks[:, tcp, :],
                                tcp == 0, tcp == tc, inc=(tcp == tc))
                P.w("vector", "pe", p_rk)
                ds1 = P.dve("tensor_single_scalar", slotf[:], ps_d[:, 0:NE],
                            float(CAP - 1), ALU.min)
                P.setrel(("rb", 3), "dve", ds1)
                P.w("vector", "dve", ds1)
                ds2 = P.dve("tensor_add", slotf[:], slotf[:], jb_sb[:])
                P.w("vector", "dve", ds2)
                ds3 = P.dve("tensor_mul", slotf[:], slotf[:], masks[:, tc, :])
                P.w("vector", "dve", ds3)
                ds4 = P.dve("reduce_sum", stf[:, 7:8], slotf[:], axis=AX.X)
                P.w("vector", "dve", ds4)
                ds5 = P.dve("tensor_copy", slotu[:, tc:tc + 1], stf[:, 7:8])
                P.w("vector", "dve", ds5)
                if tc > 0:
                    P.w("vector", "sg", c_sc)
                P.dve("tensor_scalar", slotw[:, 0:1], slotu[:, tc:tc + 1], 7, None,
                      ALU.logical_shift_right, inc=False)
                dw2 = P.dve("tensor_scalar", slotw[:, 1:2], slotu[:, tc:tc + 1], 127,
                            None, ALU.bitwise_and)
                P.w("vector", "dve", dw2)
                dw3 = P.dve("tensor_scalar", slotw[:, 1:2], slotw[:, 1:2], 5, None,
                            ALU.logical_shift_left)
                P.w("vector", "dve", dw3)
                dw4 = P.dve("tensor_tensor", slotw[:, 2:3], slotw[:, 0:1],
                            slotw[:, 1:2], ALU.bitwise_or)
                P.w("gpsimd", "dve", dw4)
                P.w("gpsimd", "sx", prelude_sx)
                c_sc = P.idma(gl_d, bass.IndirectOffsetOnAxis(slotw[:, 2:3], 0),
                              iota_sb[:, tc:tc + 1], None)
                if tc % 4 == 3:
                    trc = tc // 4
                    P.w("scalar", "act", d_e)
                    c_xnT = P.dma(
                        "scalar",
                        xnT_d.rearrange("(c p) t -> p c t", p=128)[:, :, trc * 512:(trc + 1) * 512],
                        big[:].rearrange("p (c t) -> p c t", t=2048)[:, :, trc * 512:(trc + 1) * 512],
                        "sc")

            stage = (smb[:, 512:1024], smb[:, 1024:1536])
            nst = 0
            p_last = 0
            d_e_final = P.c["act"]
            for hq in range(4):
                half = hq % 2
                base = half * 8192
                cols = 4096 + hq * 256
                P.wrel("gpsimd", ("auxh", half))
                c_wkq = P.dma("gpsimd",
                              aux[:, base:base + 8192].rearrange("p (c n) -> p c n", n=256),
                              wqkv_d[:, cols:cols + 256].rearrange("(c p) n -> p c n", p=128),
                              "sg")
                for hh in range(2):
                    h = hq * 2 + hh
                    for trng in range(NTR):
                        par = nst % 2
                        P.w("tensor", "sg", c_wkq)
                        P.w("tensor", "act", d_e_final)
                        P.wrel("tensor", ("rb", par))
                        p_last = None
                        for kc in range(KC):
                            p_last = P.mm(
                                ps_ab[par][:],
                                aux[:, base + kc * 256 + hh * 128: base + kc * 256 + hh * 128 + 128],
                                big[:, kc * 2048 + trng * 512: kc * 2048 + (trng + 1) * 512],
                                kc == 0, kc == KC - 1, inc=(kc == KC - 1))
                        spar = nst % 2
                        nst += 1
                        P.w("vector", "pe", p_last)
                        P.wcur("vector", "sta" if spar == 0 else "stb")
                        d_k = P.dve("tensor_copy", stage[spar], ps_ab[par][:])
                        P.setrel(("rb", par), "dve", d_k)
                        P.w("sync", "dve", d_k)
                        c_st = P.dma("sync",
                                     kT_d[h * 128:(h + 1) * 128, trng * 512:(trng + 1) * 512],
                                     stage[spar], "sta" if spar == 0 else "stb")
                P.setrel(("auxh", half), "pe", p_last)
            for q4 in range(2, 4):
                P.wrel("gpsimd", ("auxh", 0))
                P.wrel("gpsimd", ("auxh", 1))
                c_wkq = P.dma("gpsimd", aux[:].rearrange("p (c n) -> p c n", n=512),
                              wqkv_d[:, 4096 + q4 * 512:4096 + (q4 + 1) * 512]
                              .rearrange("(c p) n -> p c n", p=128),
                              "sg")
                for tc in range(NTC):
                    par = nst % 2
                    P.w("tensor", "sg", c_wkq)
                    P.w("tensor", "act", d_e_final)
                    P.wrel("tensor", ("rb", par))
                    p_last = None
                    for kc in range(KC):
                        p_last = P.mm(
                            ps_ab[par][:],
                            big[:, kc * 2048 + tc * 128: kc * 2048 + tc * 128 + 128],
                            aux[:, kc * 512:(kc + 1) * 512],
                            kc == 0, kc == KC - 1, inc=(kc == KC - 1))
                    spar = nst % 2
                    nst += 1
                    P.w("vector", "pe", p_last)
                    P.wcur("vector", "sta" if spar == 0 else "stb")
                    d_v = P.dve("tensor_copy", stage[spar], ps_ab[par][:])
                    P.setrel(("rb", par), "dve", d_v)
                    P.w("sync", "dve", d_v)
                    c_sv = P.dma("sync",
                                 v_d[tc * 128:(tc + 1) * 128, (q4 - 2) * 512:(q4 - 1) * 512],
                                 stage[spar], "sta" if spar == 0 else "stb")
                P.setrel(("auxh", 0), "pe", p_last)
                P.setrel(("auxh", 1), "pe", p_last)
            for g in range(GQ):
                P.setrel(("wq", g), "pe", p_last)

            P.setrel(("wo", 0), "sc", c_xnT)
            P.setrel(("wo", 1), "sc", c_xnT)
            P.setrel("otn_region", "sc", c_xnT)
            if DEBUG:
                P.w("sync", "dve", P.c["dve"])
                P.dma("sync", masks_d, masks[:], "out")

            if PHASES < 2:
                fin(P)
                return

            def xsl(kc):
                return big[:, kc * 512:(kc + 1) * 512]

            def otn(h, lo, w):
                return big[:, 16384 + h * 512 + lo: 16384 + h * 512 + lo + w]

            def wosl(wpar, kc):
                o = 32768 + wpar * 16384 + kc * 512
                return big[:, o:o + 512]

            qT = smb[:, 0:512]
            pt = (smb[:, 512:1024], smb[:, 1024:1536])
            ex = (smb[:, 1536:2048], smb[:, 2048:2560])
            recb = smb[:, 2560:3072]
            sst = (smb[:, 3072:3584], smb[:, 3584:4096])
            kT_sb = kvb[:, 0:2048]
            sqd_sb = shf[:, 0:MCOL]
            recf = shf[0:1, 3584:4096]

            P.w("sync", "dve", x_users[0])
            c_sqd = P.dma("sync", sqd_sb, sqd_d, "sx")

            first_otn = True
            p_q = 0
            for trng in range(NTR):
                KT = 4 * (trng + 1)
                kend = 512 * (trng + 1)
                if trng == 0:
                    P.w("sync", "sc", c_xnT)
                else:
                    P.wrel("sync", "xnT_sl")
                c_xsl = P.dma("sync",
                              big[:, 0:16384].rearrange("p (c t) -> p c t", t=512),
                              xnT_d.rearrange("(c p) t -> p c t", p=128)[:, :, trng * 512:(trng + 1) * 512],
                              "sx")
                for h in range(NKV):
                    P.wrel("sync", "kT_sb")
                    c_kT = P.dma("sync", kT_sb[:, 0:kend],
                                 kT_d[h * 128:(h + 1) * 128, 0:kend], "sx")
                    P.wrel("sync", "v_sb")
                    c_v = P.dma("sync",
                                kvb[:, 2048:2048 + 128 * KT].rearrange("p (c d) -> p c d", d=128),
                                v_d[0:kend, h * 128:(h + 1) * 128].rearrange("(c p) d -> p c d", p=128),
                                "sx")
                    c_wq = {}
                    for g in range(GQ):
                        P.wrel("gpsimd", ("wq", g))
                        c_wq[g] = P.dma(
                            "gpsimd",
                            aux[:].rearrange("p (c n) -> p c n", n=512)[:, :, g * 128:(g + 1) * 128],
                            wqkv_d[:, (h * GQ + g) * 128:(h * GQ + g + 1) * 128]
                            .rearrange("(c p) n -> p c n", p=128),
                            "sg")
                    psc = {}
                    pvv = {}
                    if P2_STAGE == "loads":
                        P.setrel("kT_sb", "sx", c_v)
                        P.setrel("v_sb", "sx", c_v)
                        continue
                    for g in range(GQ):
                        head = h * GQ + g
                        slope = float(2.0 ** (-0.25 * (head + 1)))
                        P.wrel("scalar", "metb")
                        P.w("scalar", "sx", c_sqd)
                        a_met = P.act(metb[:], sqd_sb, AF.Exp, scale=-slope)
                        P.wrel("tensor", ("rb", 2))
                        P.w("tensor", "sg", c_wq[g])
                        P.w("tensor", "sx", c_xsl)
                        for kc in range(KC):
                            p_q = P.mm(ps_e[:],
                                       aux[:, kc * 512 + g * 128: kc * 512 + g * 128 + 128],
                                       xsl(kc), kc == 0, kc == KC - 1, inc=(kc == KC - 1))
                        P.w("vector", "pe", p_q)
                        P.wrel("vector", "qT")
                        d_q = P.dve("tensor_copy", qT, ps_e[:])
                        P.setrel(("rb", 2), "dve", d_q)
                        P.setrel(("wq", g), "pe", p_q)
                        P.setrel("qT", "pe", p_q)
                        P.setrel("metb", "dve", d_q)
                        if P2_STAGE == "qgen":
                            continue

                        exv = {}
                        ptv = {}
                        psc = {}
                        pvv = {}

                        ps_sc = (ps_a, ps_b, ps_e, ps_d)

                        def em_score(kt):
                            b4 = kt % 4
                            P.w("tensor", "act", exv.get(kt - 4, 0))
                            P.wrel("tensor", ("rb", b4))
                            if kt == 0:
                                P.w("tensor", "dve", d_q)
                                P.w("tensor", "sx", c_kT)
                            return P.mm(ps_sc[b4][:], kT_sb[:, kt * 128:(kt + 1) * 128],
                                        qT, True, True, inc=True)

                        for kt0 in range(min(4, KT)):
                            psc[kt0] = em_score(kt0)
                        p_den = None
                        for kt in range(KT):
                            b4 = kt % 4
                            par = kt % 2
                            P.w("scalar", "pe", psc[kt])
                            P.w("scalar", "dve", ptv.get(kt - 2, 0))
                            exv[kt] = P.act(ex[par], ps_sc[b4][:], AF.Exp, scale=SCALE)
                            P.setrel(("rb", b4), "act", exv[kt])
                            m0 = trng * 512 - kt * 128 + 384
                            P.w("vector", "act", exv[kt])
                            P.w("vector", "pe", pvv.get(kt - 2, 0))
                            if kt == 0:
                                P.w("vector", "act", a_met)
                            ptv[kt] = P.dve("tensor_mul", pt[par], ex[par],
                                            metb[:, m0:m0 + 512])
                            if kt + 4 < KT:
                                psc[kt + 4] = em_score(kt + 4)
                            P.w("tensor", "dve", ptv[kt])
                            if kt == 0:
                                P.wrel("tensor", "ps_den")
                                P.wrel("tensor", "ps_c")
                                P.w("tensor", "sx", c_v)
                            P.mm(ps_den[:], onesc[:], pt[par], kt == 0, kt == KT - 1,
                                 inc=(kt == KT - 1))
                            if kt == KT - 1:
                                p_den = P.c["pe"]
                            pvv[kt] = P.mm(ps_c[:],
                                           kvb[:, 2048 + kt * 128:2048 + (kt + 1) * 128],
                                           pt[par], kt == 0, kt == KT - 1, inc=True)
                        P.setrel("metb", "dve", ptv[KT - 1])
                        P.setrel("qT", "pe", psc[KT - 1])
                        P.w("vector", "pe", p_den)
                        d_rc = P.dve("reciprocal", recf[:], ps_den[0:1, :])
                        P.w("vector", "dve", d_rc)
                        d_rec = P.dve("tensor_copy", recbf[:], recf[:])
                        P.setrel("ps_den", "dve", d_rec)
                        P.wrel("tensor", ("rb", 3))
                        P.w("tensor", "dve", d_rec)
                        p_bc = P.mm(ps_d[:], onesr[0:1, :], recbf[0:1, :], True, True,
                                    inc=True)
                        P.w("vector", "pe", p_bc)
                        d_recb = P.dve("tensor_copy", recb, ps_d[:])
                        P.setrel(("rb", 3), "dve", d_recb)
                        if first_otn:
                            P.wcur("vector", "sc")
                            first_otn = False
                        P.w("vector", "dve", d_recb)
                        d_otn = P.dve("tensor_mul", otn(head, 0, 512), ps_c[:], recb)
                        P.setrel("ps_c", "dve", d_otn)
                    if psc:
                        P.setrel("kT_sb", "pe", psc[KT - 1])
                        P.setrel("v_sb", "pe", pvv[KT - 1])
                    elif P2_STAGE == "qgen":
                        P.setrel("kT_sb", "sx", c_v)
                        P.setrel("v_sb", "sx", c_v)
                if P2_STAGE != "full":
                    P.setrel("xnT_sl", "pe", p_q)
                    continue
                d_otn_all = P.c["dve"]
                p_s = 0
                for fo in range(NOR):
                    wpar = fo % 2
                    if trng == 0 and fo < 2:
                        P.wcur("gpsimd", "sc")
                    P.wrel("gpsimd", ("wo", wpar))
                    c_wo = P.dma("gpsimd",
                                 big[:, 32768 + wpar * 16384:32768 + (wpar + 1) * 16384]
                                 .rearrange("p (c n) -> p c n", n=512),
                                 wo_d[:, fo * 512:(fo + 1) * 512].rearrange("(c p) n -> p c n", p=128),
                                 "sg")
                    for tt in range(4):
                        par = tt % 2
                        P.w("tensor", "sg", c_wo)
                        if fo == 0 and tt == 0:
                            P.w("tensor", "dve", d_otn_all)
                        P.wrel("tensor", ("rb", par))
                        for hc in range(KC):
                            p_s = P.mm(ps_ab[par][:], otn(hc, tt * 128, 128), wosl(wpar, hc),
                                       hc == 0, hc == KC - 1, inc=(hc == KC - 1))
                        if tt == 3:
                            P.setrel(("wo", wpar), "pe", p_s)
                        P.w("vector", "pe", p_s)
                        P.wcur("vector", "ssa" if par == 0 else "ssb")
                        d_s = P.dve("tensor_copy", sst[par], ps_ab[par][:])
                        P.setrel(("rb", par), "dve", d_s)
                        P.w("scalar", "dve", d_s)
                        c_ss = P.dma("scalar",
                                     sT_d[trng * 512 + tt * 128: trng * 512 + (tt + 1) * 128,
                                          fo * 512:(fo + 1) * 512],
                                     sst[par], "ssa" if par == 0 else "ssb")
                P.setrel("xnT_sl", "pe", p_q)

            c_sT_done = P.c["sx"]

            if PHASES < 4:
                fin(P)
                return

            def wjsl(wpar, kc):
                o = 24576 + wpar * 16896 + kc * 512
                return big[:, o:o + 512]

            def xgt(gpar):
                return big[:, gpar * 4096:(gpar + 1) * 4096]

            def xntg(kc, lo, w):
                return big[:, 8192 + kc * 512 + lo: 8192 + kc * 512 + lo + w]

            ps_rot = (ps_a, ps_b, ps_e)
            for rpar, src in ((0, ("rb", 0)), (1, ("rb", 1)), (2, ("rb", 2))):
                if src in P.rel:
                    P.rel[("ps_rot", rpar)] = P.rel[src]
            last_wo_pe = P.c["pe"]
            last_p23_dve = P.c["dve"]
            P.setrel(("wj", 0), "pe", last_wo_pe)
            P.setrel(("wj", 1), "pe", last_wo_pe)
            nrot = 0
            nst4 = 0
            NTILE = CAP // 128

            P.w("sync", "sg", c_sc)
            c_gl = P.dma("sync", glall[:],
                         gl_d.rearrange("(p c) o -> p c o", c=32), "sx")
            for j in range(NE):
                d_xnTg = {}
                p_t = 0
                for tile in range(NTILE):
                    gpar = tile % 2
                    P.wrel("gpsimd", ("xgt", gpar))
                    P.w("gpsimd", "sx", c_gl)
                    P.w("gpsimd", "sxt", c_xt)
                    if j == 0 and tile == 0:
                        P.w("gpsimd", "pe", last_wo_pe)
                        P.w("gpsimd", "dve", last_p23_dve)
                    c_xg = P.idma(xgt(gpar), None, xt_d,
                                  bass.IndirectOffsetOnAxis(glall[:, j * 4 + tile, :], 0))
                    d_tx = None
                    for fc in range(KC):
                        tpar = fc % 2
                        if fc == 0:
                            P.w("tensor", "sg", c_xg)
                        P.wrel("tensor", ("ps_tr", tpar))
                        p_t = P.tr(ps_tr2[tpar][:], xgt(gpar)[:, fc * 128:(fc + 1) * 128],
                                   ident[:])
                        P.w("vector", "pe", p_t)
                        d_tx = P.dve("tensor_copy", xntg(fc, tile * 128, 128),
                                     ps_tr2[tpar][:])
                        P.setrel(("ps_tr", tpar), "dve", d_tx)
                    P.setrel(("xgt", gpar), "pe", p_t)
                    d_xnTg[tile] = d_tx
                for o in range(NOR):
                    wpar = (j * NOR + o) % 2
                    P.wrel("gpsimd", ("wj", wpar))
                    c_wj = P.dma("gpsimd",
                                 big[:, 24576 + wpar * 16896:24576 + wpar * 16896 + 16896]
                                 .rearrange("p (c n) -> p c n", n=512),
                                 wjT_d[j][:, o * 512:(o + 1) * 512]
                                 .rearrange("(c p) n -> p c n", p=128),
                                 "sg")
                    for tile in range(NTILE):
                        rpar = nrot % 3
                        nrot += 1
                        P.w("tensor", "sg", c_wj)
                        if o == 0:
                            P.w("tensor", "dve", d_xnTg[tile])
                        P.wrel("tensor", ("ps_rot", rpar))
                        P.mm(ps_rot[rpar][:], onesr[0:1, :], wjsl(wpar, KC)[0:1, :],
                             True, False)
                        p_x = None
                        for kc in range(KC):
                            p_x = P.mm(ps_rot[rpar][:], xntg(kc, tile * 128, 128),
                                       wjsl(wpar, kc), False, kc == KC - 1,
                                       inc=(kc == KC - 1))
                        if tile == NTILE - 1:
                            P.setrel(("wj", wpar), "pe", p_x)
                        spar = nst4 % 2
                        nst4 += 1
                        P.w("vector", "pe", p_x)
                        P.wcur("vector", "ssa" if spar == 0 else "ssb")
                        d_s = P.dve("tensor_copy", sst[spar], ps_rot[rpar][:])
                        P.setrel(("ps_rot", rpar), "dve", d_s)
                        P.w("scalar", "dve", d_s)
                        c_f = P.dma("scalar",
                                    ffn_d[j * CAP + tile * 128: j * CAP + (tile + 1) * 128,
                                          o * 512:(o + 1) * 512],
                                    sst[spar], "ssa" if spar == 0 else "ssb")
            bigf = big[:].bitcast(F32)

            def xrow(par):
                return bigf[:, 8192 + par * 4096: 8192 + (par + 1) * 4096]

            g4d = (metb[:, 0:512], metb[:, 512:1024])
            c_ffn_a = P.c["ssa"]
            c_ffn_b = P.c["ssb"]
            a_sqd_users2 = P.c["act"]
            prev_g4 = {}
            for tc in range(NTC):
                fpar = tc % 2
                P.wrel("gpsimd", ("ft", fpar))
                P.w("gpsimd", "ssa", c_ffn_a)
                P.w("gpsimd", "ssb", c_ffn_b)
                c_fg = P.idma(xgt(fpar), None, ffn_d,
                              bass.IndirectOffsetOnAxis(slotu[:, tc:tc + 1], 0))
                P.wrel("sync", ("stok", fpar))
                if tc < 2:
                    P.w("sync", "pe", P.c["pe"])
                    P.w("sync", "ssa", c_ffn_a)
                    P.w("sync", "ssb", c_ffn_b)
                c_s = P.dma("sync", big[:, 8192 + fpar * 4096:8192 + (fpar + 1) * 4096],
                            sT_d[tc * 128:(tc + 1) * 128, :], "sx")
                P.wcur("sync", "ya" if fpar == 0 else "yb")
                c_x = P.dma("sync", xrow(fpar), x_d[tc * 128:(tc + 1) * 128, :], "sx")
                d_out = 0
                a_g = 0
                for o in range(NOR):
                    gp4 = o % 2
                    P.w("scalar", "sg", c_fg)
                    P.w("scalar", "dve", prev_g4.get(gp4, 0))
                    a_g = P.act(g4d[gp4], xgt(fpar)[:, o * 512:(o + 1) * 512], AF.Silu)
                    P.w("vector", "act", a_g)
                    P.w("vector", "sx", c_s)
                    d_sg = P.dve("tensor_mul", out4[:], g4d[gp4],
                                 big[:, 8192 + fpar * 4096 + o * 512:
                                     8192 + fpar * 4096 + (o + 1) * 512])
                    prev_g4[gp4] = d_sg
                    if o == 0:
                        P.w("vector", "sx", c_x)
                    P.w("vector", "dve", d_sg)
                    d_out = P.dve("tensor_add", xrow(fpar)[:, o * 512:(o + 1) * 512],
                                  xrow(fpar)[:, o * 512:(o + 1) * 512], out4[:])
                P.w("scalar", "dve", d_out)
                c_y = P.dma("scalar", y_d[tc * 128:(tc + 1) * 128, :], xrow(fpar),
                            "ya" if fpar == 0 else "yb")
                P.setrel(("ft", fpar), "act", a_g)
                P.setrel(("stok", fpar), "dve", d_sg)
            fin(P)

        @block.tensor
        def _(eng):
            body(Em(nc, "tensor", eng, sems))

        @block.vector
        def _(eng):
            body(Em(nc, "vector", eng, sems))

        @block.scalar
        def _(eng):
            body(Em(nc, "scalar", eng, sems))

        @block.gpsimd
        def _(eng):
            body(Em(nc, "gpsimd", eng, sems))

        @block.sync
        def _(eng):
            body(Em(nc, "sync", eng, sems))

    return nc


def prep_inputs(L0_attn_hidden_states, L0_attn_w_qkv, L0_attn_w_o,
                L0_ffn_gate_w, L0_ffn_gate_b, L0_ffn_expert_w, L0_ffn_expert_b):
    x = np.asarray(L0_attn_hidden_states, dtype=np.float32)
    w_qkv = np.ascontiguousarray(np.asarray(L0_attn_w_qkv, dtype=np.float32))
    w_o = np.ascontiguousarray(np.asarray(L0_attn_w_o, dtype=np.float32))
    gw = np.asarray(L0_ffn_gate_w, dtype=np.float32)
    gwT = gw.T.reshape(KC, 128, NE).transpose(1, 0, 2)
    gbpad = np.zeros((128, 1, NE), dtype=np.float32)
    gbpad[0, 0, :] = np.asarray(L0_ffn_gate_b, dtype=np.float32)
    gwT = np.ascontiguousarray(np.concatenate([gwT, gbpad], axis=1))
    wjT = np.asarray(L0_ffn_expert_w, dtype=np.float32).transpose(0, 2, 1)
    ebpad = np.zeros((NE, 128, DM), dtype=np.float32)
    ebpad[:, 0, :] = np.asarray(L0_ffn_expert_b, dtype=np.float32)
    wjT = np.ascontiguousarray(np.concatenate([wjT, ebpad], axis=1))

    lte = (np.arange(128)[:, None] < np.arange(128)[None, :])
    ltones = np.concatenate([lte, np.ones((128, 128))], axis=1).astype(ml_dtypes.bfloat16)
    jb = np.broadcast_to((np.arange(NE) * CAP).astype(np.float32), (128, NE)).copy()
    iota16 = (np.arange(NTC)[None, :] * 128 + np.arange(128)[:, None]).astype(np.uint32)

    ki = np.arange(128)[:, None]
    m = np.arange(MCOL)[None, :]
    d = m - 384 - ki
    sqd = np.where(d >= 0, np.sqrt(np.maximum(d, 0).astype(np.float64)), 1e6).astype(np.float32)

    ident = np.eye(128).astype(ml_dtypes.bfloat16)
    ones_row = np.ones((1, 128), dtype=ml_dtypes.bfloat16)
    ones_col = np.ones((128, 1), dtype=ml_dtypes.bfloat16)

    shared = {"w_qkv": w_qkv, "w_o": w_o, "gwT": gwT,
              "wjT": wjT, "sqd": sqd, "ident_i": ident,
              "ones_row_i": ones_row, "ones_col_i": ones_col,
              "ltones": ltones, "jb": jb, "iota16": iota16}
    in_maps = []
    for b in range(8):
        im = dict(shared)
        im["x"] = np.ascontiguousarray(x[b])
        in_maps.append(im)
    return in_maps


def kernel(**inputs):
    in_maps = prep_inputs(**inputs)
    nc = build_nc()
    res = run_bass_kernel_spmd(nc, in_maps, core_ids=list(range(8)))
    out = np.stack([np.asarray(r["y"]) for r in res.results], axis=0)
    return out


# version 14
# speedup vs baseline: 1.1253x; 1.0340x over previous
import numpy as np
import ml_dtypes
from contextlib import ExitStack

import concourse.bass as bass
import concourse.mybir as mybir
from concourse.bass_utils import run_bass_kernel_spmd

F32 = mybir.dt.float32
BF16 = mybir.dt.bfloat16
AF = mybir.ActivationFunctionType
ALU = mybir.AluOpType
AX = mybir.AxisListType

T = 2048
DM = 4096
KC = 32
NTC = 16
NTR = 4
NKV = 8
GQ = 4
NE = 8
NOR = 8
SCALE = float(128.0 ** -0.5)
EPS = 1e-5
MCOL = 2432

CAP = 384
NSLOT = NE * CAP
GLROWS = 4096

PHASES = 4
DEBUG = False
P2_STAGE = "full"


class Em:

    def __init__(self, nc, role, eng, sem):
        self.nc = nc
        self.role = role
        self.eng = eng
        self.sem = sem
        self.c = dict(pe=0, act=0, dve=0, sx=0, sc=0, sg=0, out=0,
                      sta=0, stb=0, ssa=0, ssb=0, ya=0, yb=0, sxt=0)
        self.rel = {}

    def w(self, role, sname, val):
        if val is None or val <= 0:
            return
        if self.role == role:
            self.eng.wait_ge(self.sem[sname], val)

    def wcur(self, role, sname):
        self.w(role, sname, self.c[sname])

    def wrel(self, role, key):
        r = self.rel.get(key)
        if r is not None:
            self.w(role, r[0], r[1])

    def setrel(self, key, sname, val):
        self.rel[key] = (sname, val)

    def _inc(self, inst, sname, n):
        if sname is None:
            return None
        self.c[sname] += n
        if inst is not None:
            inst.then_inc(self.sem[sname], n)
        return self.c[sname]

    def mm(self, out, lhsT, rhs, start, stop, inc=False):
        inst = None
        if self.role == "tensor":
            inst = self.nc.tensor.matmul(out, lhsT, rhs, start=start, stop=stop,
                                         skip_group_check=True)
        return self._inc(inst, "pe" if inc else None, 1)

    def tr(self, out, in_, ident, inc=True):
        inst = None
        if self.role == "tensor":
            inst = self.nc.tensor.transpose(out, in_, ident)
        return self._inc(inst, "pe" if inc else None, 1)

    def act(self, out, in_, func, bias=0.0, scale=1.0, accum_out=None, inc=True):
        inst = None
        if self.role == "scalar":
            inst = self.nc.scalar.activation(out, in_, func, bias=bias, scale=scale,
                                             accum_out=accum_out)
        return self._inc(inst, "act" if inc else None, 1)

    def dve(self, method, *args, inc=True, **kw):
        inst = None
        if self.role == "vector":
            inst = getattr(self.nc.vector, method)(*args, **kw)
        return self._inc(inst, "dve" if inc else None, 1)

    def dma(self, qrole, out, in_, sname):
        inst = None
        if self.role == qrole:
            eng = {"sync": self.nc.sync, "scalar": self.nc.scalar,
                   "gpsimd": self.nc.gpsimd}[qrole]
            inst = eng.dma_start(out=out, in_=in_)
        return self._inc(inst, sname, 16)

    def idma(self, out, out_off, in_, in_off):
        inst = None
        if self.role == "gpsimd":
            inst = self.nc.gpsimd.indirect_dma_start(out, out_off, in_, in_off)
        return self._inc(inst, "sg", 16)


def build_nc():
    nc = bass.Bass("TRN2", target_bir_lowering=False)

    x_d = nc.dram_tensor("x", [T, DM], F32, kind="ExternalInput").ap()
    wqkv_d = nc.dram_tensor("w_qkv", [DM, 6144], F32, kind="ExternalInput").ap()
    wo_d = nc.dram_tensor("w_o", [DM, DM], F32, kind="ExternalInput").ap()
    gwT_d = nc.dram_tensor("gwT", [128, KC + 1, NE], F32, kind="ExternalInput").ap()
    wjT_d = nc.dram_tensor("wjT", [NE, DM + 128, DM], F32, kind="ExternalInput").ap()
    sqd_d = nc.dram_tensor("sqd", [128, MCOL], F32, kind="ExternalInput").ap()
    lt_d = nc.dram_tensor("ltones", [128, 256], BF16, kind="ExternalInput").ap()
    jb_d = nc.dram_tensor("jb", [128, NE], F32, kind="ExternalInput").ap()
    iota_d = nc.dram_tensor("iota16", [128, NTC], mybir.dt.uint32, kind="ExternalInput").ap()
    ident_d = nc.dram_tensor("ident_i", [128, 128], BF16, kind="ExternalInput").ap()
    onesr_d = nc.dram_tensor("ones_row_i", [1, 128], BF16, kind="ExternalInput").ap()
    onesc_d = nc.dram_tensor("ones_col_i", [128, 1], BF16, kind="ExternalInput").ap()

    y_d = nc.dram_tensor("y", [T, DM], F32, kind="ExternalOutput").ap()
    ikind = "ExternalOutput" if DEBUG else "Internal"
    xnT_d = nc.dram_tensor("xnT_s", [DM, T], BF16, kind=ikind).ap()
    kT_d = nc.dram_tensor("kT_s", [NKV * 128, T], BF16, kind=ikind).ap()
    v_d = nc.dram_tensor("v_s", [T, NKV * 128], BF16, kind=ikind).ap()
    sT_d = nc.dram_tensor("sT_s", [T, DM], BF16, kind=ikind).ap()
    xt_d = nc.dram_tensor("xtok_s", [T, DM], BF16, kind="Internal").ap()
    ffn_d = nc.dram_tensor("ffn_s", [NSLOT, DM], BF16, kind=ikind).ap()
    gl_d = nc.dram_tensor("glist_s", [GLROWS, 1], mybir.dt.uint32, kind=ikind).ap()
    masks_d = (nc.dram_tensor("masks_s", [128, NTC, NE], BF16, kind="ExternalOutput").ap()
               if DEBUG else None)

    with ExitStack() as st:
        E = st.enter_context
        big = E(nc.sbuf_tensor("big", [128, 65536], BF16))
        aux = E(nc.sbuf_tensor("aux", [128, 16384], BF16))
        kvb = E(nc.sbuf_tensor("kvb", [128, 4096], BF16))
        shf = E(nc.sbuf_tensor("shf", [128, 4096], F32))
        metb = E(nc.sbuf_tensor("metb", [128, MCOL], BF16))
        smb = E(nc.sbuf_tensor("smb", [128, 4096], BF16))
        stf = E(nc.sbuf_tensor("stf", [128, 8], F32))
        masks = E(nc.sbuf_tensor("masks", [128, NTC, NE], BF16))
        lt_sb = E(nc.sbuf_tensor("lt_sb", [128, 256], BF16))
        jb_sb = E(nc.sbuf_tensor("jb_sb", [128, NE], F32))
        slotf = E(nc.sbuf_tensor("slotf", [128, NE], F32))
        slotu = E(nc.sbuf_tensor("slotu", [128, NTC], mybir.dt.uint32))
        iota_sb = E(nc.sbuf_tensor("iota_sb", [128, NTC], mybir.dt.uint32))
        zsb = E(nc.sbuf_tensor("zsb", [128, 32], mybir.dt.uint32))
        slotw = E(nc.sbuf_tensor("slotw", [128, 4], mybir.dt.uint32))
        glall = E(nc.sbuf_tensor("glall", [128, 32, 1], mybir.dt.uint32))
        recbf = E(nc.sbuf_tensor("recbf", [1, 512], BF16))
        out4 = E(nc.sbuf_tensor("out4", [128, 512], F32))
        gw_bf = E(nc.sbuf_tensor("gw_bf", [128, (KC + 1) * NE], BF16))
        ident = E(nc.sbuf_tensor("ident", [128, 128], BF16))
        onesr = E(nc.sbuf_tensor("onesr", [1, 128], BF16))
        onesc = E(nc.sbuf_tensor("onesc", [128, 1], BF16))

        ps_a = E(nc.psum_tensor("ps_a", [128, 512], F32))
        ps_b = E(nc.psum_tensor("ps_b", [128, 512], F32))
        ps_c = E(nc.psum_tensor("ps_c", [128, 512], F32))
        ps_d = E(nc.psum_tensor("ps_d", [128, 512], F32))
        ps_e = E(nc.psum_tensor("ps_e", [128, 512], F32))
        ps_den = E(nc.psum_tensor("ps_den", [1, 512], F32))
        ps_t0 = E(nc.psum_tensor("ps_t0", [128, 128], BF16))
        ps_t1 = E(nc.psum_tensor("ps_t1", [128, 128], BF16))

        sems = {}
        for sn in ("pe", "act", "dve", "sx", "sc", "sg", "out",
                   "sta", "stb", "ssa", "ssb", "ya", "yb", "sxt"):
            sems[sn] = E(nc.semaphore(sn))
        block = E(nc.Block())

        def fin(P: Em):
            for sn in ("out", "sta", "stb", "ssa", "ssb", "ya", "yb", "sxt"):
                P.w("sync", sn, P.c[sn])
            P.w("sync", "sx", P.c["sx"])
            P.w("sync", "sc", P.c["sc"])
            P.w("sync", "sg", P.c["sg"])
            P.w("sync", "pe", P.c["pe"])
            P.w("sync", "dve", P.c["dve"])
            P.w("sync", "act", P.c["act"])

        def body(P: Em):
            ps_ab = (ps_a, ps_b)
            ps_tr2 = (ps_t0, ps_t1)

            P.dma("sync", ident[:], ident_d, "sx")
            c_onesr = P.dma("sync", onesr[:], onesr_d, "sx")
            P.dma("sync", onesc[:], onesc_d, "sx")
            c_gwbf = P.dma("gpsimd", gw_bf[:], gwT_d.rearrange("p c e -> p (c e)"), "sg")
            P.dma("sync", lt_sb[:], lt_d, "sx")
            P.dma("sync", jb_sb[:], jb_d, "sx")
            c_iota = P.dma("sync", iota_sb[:], iota_d, "sx")
            d_z = P.dve("memset", zsb[:], 0)
            P.w("sync", "dve", d_z)
            c_glz = P.dma("sync", gl_d.rearrange("(p c) o -> p (c o)", c=32),
                          zsb[:], "sx")

            x_users = None
            kvb_users_pe = 0
            a_sq = 0

            for tc in range(NTC):
                if x_users is not None:
                    P.w("sync", "dve", x_users[0])
                c_x = P.dma("sync", shf[:], x_d[tc * 128:(tc + 1) * 128, :], "sx")
                if tc == 0:
                    prelude_sx = c_x
                P.w("vector", "sx", c_x)
                d1 = P.dve("reduce_sum", stf[:, 0:1], shf[:], axis=AX.X)
                P.w("scalar", "sx", c_x)
                a_sq = P.act(aux[:, 0:4096], shf[:], AF.Square, accum_out=stf[:, 1:2])
                P.setrel("aux", "act", a_sq)
                P.w("vector", "act", a_sq)
                P.w("vector", "dve", d1)
                d2 = P.dve("tensor_scalar_mul", stf[:, 2:3], stf[:, 0:1], 1.0 / DM)
                P.dve("tensor_scalar_mul", stf[:, 3:4], stf[:, 1:2], 1.0 / DM, inc=False)
                P.w("vector", "dve", d2)
                d4 = P.dve("tensor_mul", stf[:, 5:6], stf[:, 2:3], stf[:, 2:3])
                P.w("vector", "dve", d4)
                d5 = P.dve("tensor_sub", stf[:, 3:4], stf[:, 3:4], stf[:, 5:6])
                P.w("vector", "dve", d5)
                d_var = P.dve("tensor_scalar_add", stf[:, 3:4], stf[:, 3:4], EPS)
                P.w("scalar", "dve", d_var)
                a_ln = P.act(stf[:, 5:6], stf[:, 3:4], AF.Ln)
                P.w("scalar", "act", a_ln)
                a_rstd = P.act(stf[:, 4:5], stf[:, 5:6], AF.Exp, scale=-0.5)
                P.w("vector", "act", a_rstd)
                P.w("vector", "pe", kvb_users_pe)
                P.wcur("vector", "sxt")
                d_xn = P.dve("tensor_scalar", kvb[:, 0:4096], shf[:], stf[:, 2:3],
                             stf[:, 4:5], ALU.subtract, ALU.mult)
                x_users = (d_xn, d_xn)
                P.w("scalar", "dve", d_xn)
                c_xt = P.dma("scalar", xt_d[tc * 128:(tc + 1) * 128, :], kvb[:, 0:4096], "sxt")
                d_e = 0
                for fc in range(KC):
                    par = fc % 2
                    P.w("tensor", "dve", d_xn)
                    P.wrel("tensor", ("ps_tr", par))
                    p_t = P.tr(ps_tr2[par][:], kvb[:, fc * 128:(fc + 1) * 128], ident[:])
                    if fc == KC - 1:
                        kvb_users_pe = p_t
                    P.w("scalar", "pe", p_t)
                    d_e = P.act(big[:, fc * 2048 + tc * 128: fc * 2048 + tc * 128 + 128],
                                ps_tr2[par][:], AF.Identity)
                    P.setrel(("ps_tr", par), "act", d_e)
                P.w("tensor", "act", d_e)
                P.wrel("tensor", ("rb", 2))
                P.w("tensor", "sg", c_gwbf)
                P.mm(ps_e[:, 0:NE], onesr[0:1, :], gw_bf[0:1, KC * NE:(KC + 1) * NE],
                     True, False)
                p_g = None
                for kc in range(KC):
                    p_g = P.mm(ps_e[:, 0:NE],
                               big[:, kc * 2048 + tc * 128: kc * 2048 + tc * 128 + 128],
                               gw_bf[:, kc * NE:(kc + 1) * NE], False, kc == KC - 1,
                               inc=(kc == KC - 1))
                P.w("vector", "pe", p_g)
                d_gm = P.dve("reduce_max", stf[:, 7:8], ps_e[:, 0:NE], axis=AX.X)
                P.w("vector", "dve", d_gm)
                d_m = P.dve("tensor_scalar", masks[:, tc, :], ps_e[:, 0:NE],
                            stf[:, 7:8], 0.0, ALU.subtract, ALU.is_ge)
                P.setrel(("rb", 2), "dve", d_m)
                P.wrel("tensor", ("rb", 3))
                P.w("tensor", "dve", d_m)
                p_rk = None
                for tcp in range(tc + 1):
                    lhs = lt_sb[:, 0:128] if tcp == tc else lt_sb[:, 128:256]
                    p_rk = P.mm(ps_d[:, 0:NE], lhs, masks[:, tcp, :],
                                tcp == 0, tcp == tc, inc=(tcp == tc))
                P.w("vector", "pe", p_rk)
                ds1 = P.dve("tensor_single_scalar", slotf[:], ps_d[:, 0:NE],
                            float(CAP - 1), ALU.min)
                P.setrel(("rb", 3), "dve", ds1)
                P.w("vector", "dve", ds1)
                ds2 = P.dve("tensor_add", slotf[:], slotf[:], jb_sb[:])
                P.w("vector", "dve", ds2)
                ds3 = P.dve("tensor_mul", slotf[:], slotf[:], masks[:, tc, :])
                P.w("vector", "dve", ds3)
                ds4 = P.dve("reduce_sum", stf[:, 7:8], slotf[:], axis=AX.X)
                P.w("vector", "dve", ds4)
                ds5 = P.dve("tensor_copy", slotu[:, tc:tc + 1], stf[:, 7:8])
                P.w("vector", "dve", ds5)
                if tc > 0:
                    P.w("vector", "sg", c_sc)
                P.dve("tensor_scalar", slotw[:, 0:1], slotu[:, tc:tc + 1], 7, None,
                      ALU.logical_shift_right, inc=False)
                dw2 = P.dve("tensor_scalar", slotw[:, 1:2], slotu[:, tc:tc + 1], 127,
                            None, ALU.bitwise_and)
                P.w("vector", "dve", dw2)
                dw3 = P.dve("tensor_scalar", slotw[:, 1:2], slotw[:, 1:2], 5, None,
                            ALU.logical_shift_left)
                P.w("vector", "dve", dw3)
                dw4 = P.dve("tensor_tensor", slotw[:, 2:3], slotw[:, 0:1],
                            slotw[:, 1:2], ALU.bitwise_or)
                P.w("gpsimd", "dve", dw4)
                P.w("gpsimd", "sx", prelude_sx)
                c_sc = P.idma(gl_d, bass.IndirectOffsetOnAxis(slotw[:, 2:3], 0),
                              iota_sb[:, tc:tc + 1], None)
                if tc % 4 == 3:
                    trc = tc // 4
                    P.w("scalar", "act", d_e)
                    c_xnT = P.dma(
                        "scalar",
                        xnT_d.rearrange("(c p) t -> p c t", p=128)[:, :, trc * 512:(trc + 1) * 512],
                        big[:].rearrange("p (c t) -> p c t", t=2048)[:, :, trc * 512:(trc + 1) * 512],
                        "sc")

            stage = (smb[:, 512:1024], smb[:, 1024:1536])
            nst = 0
            p_last = 0
            d_e_final = P.c["act"]
            for hq in range(4):
                half = hq % 2
                base = half * 8192
                cols = 4096 + hq * 256
                P.wrel("gpsimd", ("auxh", half))
                c_wkq = P.dma("gpsimd",
                              aux[:, base:base + 8192].rearrange("p (c n) -> p c n", n=256),
                              wqkv_d[:, cols:cols + 256].rearrange("(c p) n -> p c n", p=128),
                              "sg")
                for hh in range(2):
                    h = hq * 2 + hh
                    for trng in range(NTR):
                        par = nst % 2
                        P.w("tensor", "sg", c_wkq)
                        P.w("tensor", "act", d_e_final)
                        P.wrel("tensor", ("rb", par))
                        p_last = None
                        for kc in range(KC):
                            p_last = P.mm(
                                ps_ab[par][:],
                                aux[:, base + kc * 256 + hh * 128: base + kc * 256 + hh * 128 + 128],
                                big[:, kc * 2048 + trng * 512: kc * 2048 + (trng + 1) * 512],
                                kc == 0, kc == KC - 1, inc=(kc == KC - 1))
                        spar = nst % 2
                        nst += 1
                        P.w("vector", "pe", p_last)
                        P.wcur("vector", "sta" if spar == 0 else "stb")
                        d_k = P.dve("tensor_copy", stage[spar], ps_ab[par][:])
                        P.setrel(("rb", par), "dve", d_k)
                        P.w("sync", "dve", d_k)
                        c_st = P.dma("sync",
                                     kT_d[h * 128:(h + 1) * 128, trng * 512:(trng + 1) * 512],
                                     stage[spar], "sta" if spar == 0 else "stb")
                P.setrel(("auxh", half), "pe", p_last)
            for q4 in range(2, 4):
                P.wrel("gpsimd", ("auxh", 0))
                P.wrel("gpsimd", ("auxh", 1))
                c_wkq = P.dma("gpsimd", aux[:].rearrange("p (c n) -> p c n", n=512),
                              wqkv_d[:, 4096 + q4 * 512:4096 + (q4 + 1) * 512]
                              .rearrange("(c p) n -> p c n", p=128),
                              "sg")
                for tc in range(NTC):
                    par = nst % 2
                    P.w("tensor", "sg", c_wkq)
                    P.w("tensor", "act", d_e_final)
                    P.wrel("tensor", ("rb", par))
                    p_last = None
                    for kc in range(KC):
                        p_last = P.mm(
                            ps_ab[par][:],
                            big[:, kc * 2048 + tc * 128: kc * 2048 + tc * 128 + 128],
                            aux[:, kc * 512:(kc + 1) * 512],
                            kc == 0, kc == KC - 1, inc=(kc == KC - 1))
                    spar = nst % 2
                    nst += 1
                    P.w("vector", "pe", p_last)
                    P.wcur("vector", "sta" if spar == 0 else "stb")
                    d_v = P.dve("tensor_copy", stage[spar], ps_ab[par][:])
                    P.setrel(("rb", par), "dve", d_v)
                    P.w("sync", "dve", d_v)
                    c_sv = P.dma("sync",
                                 v_d[tc * 128:(tc + 1) * 128, (q4 - 2) * 512:(q4 - 1) * 512],
                                 stage[spar], "sta" if spar == 0 else "stb")
                P.setrel(("auxh", 0), "pe", p_last)
                P.setrel(("auxh", 1), "pe", p_last)
            for g in range(GQ):
                P.setrel(("wq", g), "pe", p_last)

            P.setrel(("wo", 0), "sc", c_xnT)
            P.setrel(("wo", 1), "sc", c_xnT)
            P.setrel("otn_region", "sc", c_xnT)
            if DEBUG:
                P.w("sync", "dve", P.c["dve"])
                P.dma("sync", masks_d, masks[:], "out")

            if PHASES < 2:
                fin(P)
                return

            def xsl(kc):
                return big[:, kc * 512:(kc + 1) * 512]

            def otn(h, lo, w):
                return big[:, 16384 + h * 512 + lo: 16384 + h * 512 + lo + w]

            def wosl(wpar, kc):
                o = 32768 + wpar * 16384 + kc * 512
                return big[:, o:o + 512]

            qT = smb[:, 0:512]
            pt = (smb[:, 512:1024], smb[:, 1024:1536])
            ex = (smb[:, 1536:2048], smb[:, 2048:2560])
            recb = smb[:, 2560:3072]
            sst = (smb[:, 3072:3584], smb[:, 3584:4096])
            kT_sb = kvb[:, 0:2048]
            sqd_sb = shf[:, 0:MCOL]
            recf = shf[0:1, 3584:4096]

            P.w("sync", "dve", x_users[0])
            c_sqd = P.dma("sync", sqd_sb, sqd_d, "sx")

            first_otn = True
            p_q = 0
            for trng in range(NTR):
                KT = 4 * (trng + 1)
                kend = 512 * (trng + 1)
                if trng == 0:
                    P.w("sync", "sc", c_xnT)
                else:
                    P.wrel("sync", "xnT_sl")
                c_xsl = P.dma("sync",
                              big[:, 0:16384].rearrange("p (c t) -> p c t", t=512),
                              xnT_d.rearrange("(c p) t -> p c t", p=128)[:, :, trng * 512:(trng + 1) * 512],
                              "sx")
                for h in range(NKV):
                    P.wrel("sync", "kT_sb")
                    c_kT = P.dma("sync", kT_sb[:, 0:kend],
                                 kT_d[h * 128:(h + 1) * 128, 0:kend], "sx")
                    P.wrel("sync", "v_sb")
                    c_v = P.dma("sync",
                                kvb[:, 2048:2048 + 128 * KT].rearrange("p (c d) -> p c d", d=128),
                                v_d[0:kend, h * 128:(h + 1) * 128].rearrange("(c p) d -> p c d", p=128),
                                "sx")
                    c_wq = {}
                    for g in range(GQ):
                        P.wrel("gpsimd", ("wq", g))
                        c_wq[g] = P.dma(
                            "gpsimd",
                            aux[:].rearrange("p (c n) -> p c n", n=512)[:, :, g * 128:(g + 1) * 128],
                            wqkv_d[:, (h * GQ + g) * 128:(h * GQ + g + 1) * 128]
                            .rearrange("(c p) n -> p c n", p=128),
                            "sg")
                    psc = {}
                    pvv = {}
                    if P2_STAGE == "loads":
                        P.setrel("kT_sb", "sx", c_v)
                        P.setrel("v_sb", "sx", c_v)
                        continue
                    for g in range(GQ):
                        head = h * GQ + g
                        slope = float(2.0 ** (-0.25 * (head + 1)))
                        P.wrel("scalar", "metb")
                        P.w("scalar", "sx", c_sqd)
                        a_met = P.act(metb[:], sqd_sb, AF.Exp, scale=-slope)
                        P.wrel("tensor", ("rb", 2))
                        P.w("tensor", "sg", c_wq[g])
                        P.w("tensor", "sx", c_xsl)
                        for kc in range(KC):
                            p_q = P.mm(ps_e[:],
                                       aux[:, kc * 512 + g * 128: kc * 512 + g * 128 + 128],
                                       xsl(kc), kc == 0, kc == KC - 1, inc=(kc == KC - 1))
                        P.w("vector", "pe", p_q)
                        P.wrel("vector", "qT")
                        d_q = P.dve("tensor_copy", qT, ps_e[:])
                        P.setrel(("rb", 2), "dve", d_q)
                        P.setrel(("wq", g), "pe", p_q)
                        P.setrel("qT", "pe", p_q)
                        P.setrel("metb", "dve", d_q)
                        if P2_STAGE == "qgen":
                            continue

                        exv = {}
                        ptv = {}
                        psc = {}
                        pvv = {}

                        ps_sc = (ps_a, ps_b, ps_e, ps_d)

                        def em_score(kt):
                            b4 = kt % 4
                            P.w("tensor", "act", exv.get(kt - 4, 0))
                            P.wrel("tensor", ("rb", b4))
                            if kt == 0:
                                P.w("tensor", "dve", d_q)
                                P.w("tensor", "sx", c_kT)
                            return P.mm(ps_sc[b4][:], kT_sb[:, kt * 128:(kt + 1) * 128],
                                        qT, True, True, inc=True)

                        for kt0 in range(min(4, KT)):
                            psc[kt0] = em_score(kt0)
                        p_den = None
                        for kt in range(KT):
                            b4 = kt % 4
                            par = kt % 2
                            P.w("scalar", "pe", psc[kt])
                            P.w("scalar", "dve", ptv.get(kt - 2, 0))
                            exv[kt] = P.act(ex[par], ps_sc[b4][:], AF.Exp, scale=SCALE)
                            P.setrel(("rb", b4), "act", exv[kt])
                            m0 = trng * 512 - kt * 128 + 384
                            P.w("vector", "act", exv[kt])
                            P.w("vector", "pe", pvv.get(kt - 2, 0))
                            if kt == 0:
                                P.w("vector", "act", a_met)
                            ptv[kt] = P.dve("tensor_mul", pt[par], ex[par],
                                            metb[:, m0:m0 + 512])
                            if kt + 4 < KT:
                                psc[kt + 4] = em_score(kt + 4)
                            P.w("tensor", "dve", ptv[kt])
                            if kt == 0:
                                P.wrel("tensor", "ps_den")
                                P.wrel("tensor", "ps_c")
                                P.w("tensor", "sx", c_v)
                            P.mm(ps_den[:], onesc[:], pt[par], kt == 0, kt == KT - 1,
                                 inc=(kt == KT - 1))
                            if kt == KT - 1:
                                p_den = P.c["pe"]
                            pvv[kt] = P.mm(ps_c[:],
                                           kvb[:, 2048 + kt * 128:2048 + (kt + 1) * 128],
                                           pt[par], kt == 0, kt == KT - 1, inc=True)
                        P.setrel("metb", "dve", ptv[KT - 1])
                        P.setrel("qT", "pe", psc[KT - 1])
                        P.w("vector", "pe", p_den)
                        d_rc = P.dve("reciprocal", recf[:], ps_den[0:1, :])
                        P.w("vector", "dve", d_rc)
                        d_rec = P.dve("tensor_copy", recbf[:], recf[:])
                        P.setrel("ps_den", "dve", d_rec)
                        P.wrel("tensor", ("rb", 3))
                        P.w("tensor", "dve", d_rec)
                        p_bc = P.mm(ps_d[:], onesr[0:1, :], recbf[0:1, :], True, True,
                                    inc=True)
                        P.w("vector", "pe", p_bc)
                        d_recb = P.dve("tensor_copy", recb, ps_d[:])
                        P.setrel(("rb", 3), "dve", d_recb)
                        if first_otn:
                            P.wcur("vector", "sc")
                            first_otn = False
                        P.w("vector", "dve", d_recb)
                        d_otn = P.dve("tensor_mul", otn(head, 0, 512), ps_c[:], recb)
                        P.setrel("ps_c", "dve", d_otn)
                    if psc:
                        P.setrel("kT_sb", "pe", psc[KT - 1])
                        P.setrel("v_sb", "pe", pvv[KT - 1])
                    elif P2_STAGE == "qgen":
                        P.setrel("kT_sb", "sx", c_v)
                        P.setrel("v_sb", "sx", c_v)
                if P2_STAGE != "full":
                    P.setrel("xnT_sl", "pe", p_q)
                    continue
                d_otn_all = P.c["dve"]
                p_s = 0
                for fo in range(NOR):
                    wpar = fo % 2
                    if trng == 0 and fo < 2:
                        P.wcur("gpsimd", "sc")
                    P.wrel("gpsimd", ("wo", wpar))
                    c_wo = P.dma("gpsimd",
                                 big[:, 32768 + wpar * 16384:32768 + (wpar + 1) * 16384]
                                 .rearrange("p (c n) -> p c n", n=512),
                                 wo_d[:, fo * 512:(fo + 1) * 512].rearrange("(c p) n -> p c n", p=128),
                                 "sg")
                    for tt in range(4):
                        par = tt % 2
                        P.w("tensor", "sg", c_wo)
                        if fo == 0 and tt == 0:
                            P.w("tensor", "dve", d_otn_all)
                        P.wrel("tensor", ("rb", par))
                        for hc in range(KC):
                            p_s = P.mm(ps_ab[par][:], otn(hc, tt * 128, 128), wosl(wpar, hc),
                                       hc == 0, hc == KC - 1, inc=(hc == KC - 1))
                        if tt == 3:
                            P.setrel(("wo", wpar), "pe", p_s)
                        P.w("vector", "pe", p_s)
                        P.wcur("vector", "ssa" if par == 0 else "ssb")
                        d_s = P.dve("tensor_copy", sst[par], ps_ab[par][:])
                        P.setrel(("rb", par), "dve", d_s)
                        P.w("scalar", "dve", d_s)
                        c_ss = P.dma("scalar",
                                     sT_d[trng * 512 + tt * 128: trng * 512 + (tt + 1) * 128,
                                          fo * 512:(fo + 1) * 512],
                                     sst[par], "ssa" if par == 0 else "ssb")
                P.setrel("xnT_sl", "pe", p_q)

            c_sT_done = P.c["sx"]

            if PHASES < 4:
                fin(P)
                return

            def wjsl(wpar, kc):
                o = 24576 + wpar * 16896 + kc * 512
                return big[:, o:o + 512]

            def xgt(gpar):
                return big[:, gpar * 4096:(gpar + 1) * 4096]

            def xntg(kc, lo, w):
                return big[:, 8192 + kc * 512 + lo: 8192 + kc * 512 + lo + w]

            ps_rot = (ps_a, ps_b, ps_e)
            for rpar, src in ((0, ("rb", 0)), (1, ("rb", 1)), (2, ("rb", 2))):
                if src in P.rel:
                    P.rel[("ps_rot", rpar)] = P.rel[src]
            last_wo_pe = P.c["pe"]
            last_p23_dve = P.c["dve"]
            P.setrel(("wj", 0), "pe", last_wo_pe)
            P.setrel(("wj", 1), "pe", last_wo_pe)
            nrot = 0
            nst4 = 0
            NTILE = CAP // 128

            P.w("sync", "sg", c_sc)
            c_gl = P.dma("sync", glall[:],
                         gl_d.rearrange("(p c) o -> p c o", c=32), "sx")
            for j in range(NE):
                d_xnTg = {}
                p_t = 0
                for tile in range(NTILE):
                    gpar = tile % 2
                    P.wrel("gpsimd", ("xgt", gpar))
                    P.w("gpsimd", "sx", c_gl)
                    P.w("gpsimd", "sxt", c_xt)
                    if j == 0 and tile == 0:
                        P.w("gpsimd", "pe", last_wo_pe)
                        P.w("gpsimd", "dve", last_p23_dve)
                    c_xg = P.idma(xgt(gpar), None, xt_d,
                                  bass.IndirectOffsetOnAxis(glall[:, j * NTILE + tile, :], 0))
                    d_tx = None
                    for fc in range(KC):
                        tpar = fc % 2
                        if fc == 0:
                            P.w("tensor", "sg", c_xg)
                        P.wrel("tensor", ("ps_tr", tpar))
                        p_t = P.tr(ps_tr2[tpar][:], xgt(gpar)[:, fc * 128:(fc + 1) * 128],
                                   ident[:])
                        P.w("vector", "pe", p_t)
                        d_tx = P.dve("tensor_copy", xntg(fc, tile * 128, 128),
                                     ps_tr2[tpar][:])
                        P.setrel(("ps_tr", tpar), "dve", d_tx)
                    P.setrel(("xgt", gpar), "pe", p_t)
                    d_xnTg[tile] = d_tx
                for o in range(NOR):
                    wpar = (j * NOR + o) % 2
                    P.wrel("gpsimd", ("wj", wpar))
                    c_wj = P.dma("gpsimd",
                                 big[:, 24576 + wpar * 16896:24576 + wpar * 16896 + 16896]
                                 .rearrange("p (c n) -> p c n", n=512),
                                 wjT_d[j][:, o * 512:(o + 1) * 512]
                                 .rearrange("(c p) n -> p c n", p=128),
                                 "sg")
                    for tile in range(NTILE):
                        rpar = nrot % 3
                        nrot += 1
                        P.w("tensor", "sg", c_wj)
                        if o == 0:
                            P.w("tensor", "dve", d_xnTg[tile])
                        P.wrel("tensor", ("ps_rot", rpar))
                        P.mm(ps_rot[rpar][:], onesr[0:1, :], wjsl(wpar, KC)[0:1, :],
                             True, False)
                        p_x = None
                        for kc in range(KC):
                            p_x = P.mm(ps_rot[rpar][:], xntg(kc, tile * 128, 128),
                                       wjsl(wpar, kc), False, kc == KC - 1,
                                       inc=(kc == KC - 1))
                        if tile == NTILE - 1:
                            P.setrel(("wj", wpar), "pe", p_x)
                        spar = nst4 % 2
                        nst4 += 1
                        P.w("vector", "pe", p_x)
                        P.wcur("vector", "ssa" if spar == 0 else "ssb")
                        d_s = P.dve("tensor_copy", sst[spar], ps_rot[rpar][:])
                        P.setrel(("ps_rot", rpar), "dve", d_s)
                        P.w("scalar", "dve", d_s)
                        c_f = P.dma("scalar",
                                    ffn_d[j * CAP + tile * 128: j * CAP + (tile + 1) * 128,
                                          o * 512:(o + 1) * 512],
                                    sst[spar], "ssa" if spar == 0 else "ssb")
            bigf = big[:].bitcast(F32)

            def xrow(par):
                return bigf[:, 8192 + par * 4096: 8192 + (par + 1) * 4096]

            g4d = (metb[:, 0:512], metb[:, 512:1024])
            c_ffn_a = P.c["ssa"]
            c_ffn_b = P.c["ssb"]
            a_sqd_users2 = P.c["act"]
            prev_g4 = {}
            for tc in range(NTC):
                fpar = tc % 2
                P.wrel("gpsimd", ("ft", fpar))
                P.w("gpsimd", "ssa", c_ffn_a)
                P.w("gpsimd", "ssb", c_ffn_b)
                c_fg = P.idma(xgt(fpar), None, ffn_d,
                              bass.IndirectOffsetOnAxis(slotu[:, tc:tc + 1], 0))
                P.wrel("sync", ("stok", fpar))
                if tc < 2:
                    P.w("sync", "pe", P.c["pe"])
                    P.w("sync", "ssa", c_ffn_a)
                    P.w("sync", "ssb", c_ffn_b)
                c_s = P.dma("sync", big[:, 8192 + fpar * 4096:8192 + (fpar + 1) * 4096],
                            sT_d[tc * 128:(tc + 1) * 128, :], "sx")
                P.wcur("sync", "ya" if fpar == 0 else "yb")
                c_x = P.dma("sync", xrow(fpar), x_d[tc * 128:(tc + 1) * 128, :], "sx")
                d_out = 0
                a_g = 0
                for o in range(NOR):
                    gp4 = o % 2
                    P.w("scalar", "sg", c_fg)
                    P.w("scalar", "dve", prev_g4.get(gp4, 0))
                    a_g = P.act(g4d[gp4], xgt(fpar)[:, o * 512:(o + 1) * 512], AF.Silu)
                    P.w("vector", "act", a_g)
                    P.w("vector", "sx", c_s)
                    d_sg = P.dve("tensor_mul", out4[:], g4d[gp4],
                                 big[:, 8192 + fpar * 4096 + o * 512:
                                     8192 + fpar * 4096 + (o + 1) * 512])
                    prev_g4[gp4] = d_sg
                    if o == 0:
                        P.w("vector", "sx", c_x)
                    P.w("vector", "dve", d_sg)
                    d_out = P.dve("tensor_add", xrow(fpar)[:, o * 512:(o + 1) * 512],
                                  xrow(fpar)[:, o * 512:(o + 1) * 512], out4[:])
                P.w("scalar", "dve", d_out)
                c_y = P.dma("scalar", y_d[tc * 128:(tc + 1) * 128, :], xrow(fpar),
                            "ya" if fpar == 0 else "yb")
                P.setrel(("ft", fpar), "act", a_g)
                P.setrel(("stok", fpar), "dve", d_sg)
            fin(P)

        @block.tensor
        def _(eng):
            body(Em(nc, "tensor", eng, sems))

        @block.vector
        def _(eng):
            body(Em(nc, "vector", eng, sems))

        @block.scalar
        def _(eng):
            body(Em(nc, "scalar", eng, sems))

        @block.gpsimd
        def _(eng):
            body(Em(nc, "gpsimd", eng, sems))

        @block.sync
        def _(eng):
            body(Em(nc, "sync", eng, sems))

    return nc


def prep_inputs(L0_attn_hidden_states, L0_attn_w_qkv, L0_attn_w_o,
                L0_ffn_gate_w, L0_ffn_gate_b, L0_ffn_expert_w, L0_ffn_expert_b):
    x = np.asarray(L0_attn_hidden_states, dtype=np.float32)
    w_qkv = np.ascontiguousarray(np.asarray(L0_attn_w_qkv, dtype=np.float32))
    w_o = np.ascontiguousarray(np.asarray(L0_attn_w_o, dtype=np.float32))
    gw = np.asarray(L0_ffn_gate_w, dtype=np.float32)
    gwT = gw.T.reshape(KC, 128, NE).transpose(1, 0, 2)
    gbpad = np.zeros((128, 1, NE), dtype=np.float32)
    gbpad[0, 0, :] = np.asarray(L0_ffn_gate_b, dtype=np.float32)
    gwT = np.ascontiguousarray(np.concatenate([gwT, gbpad], axis=1))
    wjT = np.asarray(L0_ffn_expert_w, dtype=np.float32).transpose(0, 2, 1)
    ebpad = np.zeros((NE, 128, DM), dtype=np.float32)
    ebpad[:, 0, :] = np.asarray(L0_ffn_expert_b, dtype=np.float32)
    wjT = np.ascontiguousarray(np.concatenate([wjT, ebpad], axis=1))

    lte = (np.arange(128)[:, None] < np.arange(128)[None, :])
    ltones = np.concatenate([lte, np.ones((128, 128))], axis=1).astype(ml_dtypes.bfloat16)
    jb = np.broadcast_to((np.arange(NE) * CAP).astype(np.float32), (128, NE)).copy()
    iota16 = (np.arange(NTC)[None, :] * 128 + np.arange(128)[:, None]).astype(np.uint32)

    ki = np.arange(128)[:, None]
    m = np.arange(MCOL)[None, :]
    d = m - 384 - ki
    sqd = np.where(d >= 0, np.sqrt(np.maximum(d, 0).astype(np.float64)), 1e6).astype(np.float32)

    ident = np.eye(128).astype(ml_dtypes.bfloat16)
    ones_row = np.ones((1, 128), dtype=ml_dtypes.bfloat16)
    ones_col = np.ones((128, 1), dtype=ml_dtypes.bfloat16)

    shared = {"w_qkv": w_qkv, "w_o": w_o, "gwT": gwT,
              "wjT": wjT, "sqd": sqd, "ident_i": ident,
              "ones_row_i": ones_row, "ones_col_i": ones_col,
              "ltones": ltones, "jb": jb, "iota16": iota16}
    in_maps = []
    for b in range(8):
        im = dict(shared)
        im["x"] = np.ascontiguousarray(x[b])
        in_maps.append(im)
    return in_maps


def kernel(**inputs):
    in_maps = prep_inputs(**inputs)
    nc = build_nc()
    res = run_bass_kernel_spmd(nc, in_maps, core_ids=list(range(8)))
    out = np.stack([np.asarray(r["y"]) for r in res.results], axis=0)
    return out
